# version 01
# baseline (speedup 1.0000x reference)
import numpy as np
import ml_dtypes
import concourse.bass as bass
import concourse.mybir as mybir
from concourse.bass_utils import run_bass_kernel_spmd
from contextlib import ExitStack

F32 = mybir.dt.float32
BF16 = mybir.dt.bfloat16
I8 = mybir.dt.int8
A = mybir.AluOpType
AF = mybir.ActivationFunctionType
AX = mybir.AxisListType

B, S, D, H, DK, INTER = 2, 4096, 2048, 16, 128, 8192
C = 512
NCH = S // C
NT = C // 128
EPS = 1e-5
THETA = 1e4
SBASE = 512.0
KB = 1024
NCHD = NCH
NQD = 16
NSPD = 2
DBG_NO_U = False
DBG_U_LEVEL = 3
DBG_NO_SUPD = False


def build_nc():
    nc = bass.Bass("TRN2", target_bir_lowering=False, debug=False)

    def din(name, shape, dt):
        return nc.dram_tensor(name, shape, dt, kind="ExternalInput").ap()

    x_full = din("x_full", [S, D], F32)
    x_own = din("x_own", [KB, D], F32)
    wq_d = din("wq_s", [D, 512], F32)
    wk_d = din("wk_s", [D, 512], F32)
    wv_d = din("wv_s", [D, 512], F32)
    wg_d = din("wg_s", [D, 512], F32)
    wo_d = din("wo_s", [512, D], F32)
    bq_d = din("bq_c", [128, 4], F32)
    bk_d = din("bk_c", [128, 4], F32)
    bv_d = din("bv_c", [128, 4], F32)
    bg_d = din("bg_c", [128, 4], F32)
    wgate_d = din("w_gate", [D, INTER], I8)
    wup_d = din("w_up", [D, INTER], I8)
    wdn_d = din("w_down", [INTER, D], I8)
    sg_d = din("sg_c", [128, 64], F32)
    su_d = din("su_c", [128, 64], F32)
    sdbc_d = din("sd_bc", [128, D], F32)
    bobc_d = din("bo_bc", [128, D], F32)
    rope_d = din("ropes", [128, 4 * S], F32)
    decay_d = din("decayT", [128, 4 * 896], F32)
    gq_d = din("gq_c", [128, 16], F32)
    gk_d = din("gk_c", [128, 16], F32)
    gC_d = din("gC_c", [128, 4], F32)
    cst_d = din("consts", [128, 5], F32)
    id_d = din("ident", [128, 128], BF16)
    pt_d = din("PT", [128, 128], BF16)

    y_d = nc.dram_tensor("y", [KB, D], F32, kind="ExternalOutput").ap()
    cc_in_t = nc.dram_tensor("cc_in", [S, D], F32)
    cc_out_t = nc.dram_tensor("cc_out", [KB, D], F32)
    cc_in = cc_in_t.ap()
    cc_out = cc_out_t.ap()

    with ExitStack() as st:
        E = st.enter_context
        ARENA = 190 * KB
        arena = E(nc.sbuf_tensor("arena", [128, ARENA], I8))
        cur = [0]

        def sb(cols, dt):
            nb = cols * {F32: 4, BF16: 2, I8: 1}[dt]
            nb = (nb + 511) // 512 * 512
            off = cur[0]
            cur[0] += nb
            assert cur[0] <= ARENA, f"arena overflow {cur[0]}"
            return arena[:, off:off + cols * {F32: 4, BF16: 2, I8: 1}[dt]].bitcast(dt)

        xbuf = [sb(D, F32) for _ in range(2)]
        scr = sb(D, F32)
        msum = [sb(1, F32) for _ in range(2)]
        mneg = [sb(1, F32) for _ in range(2)]
        sdv = [sb(1, F32) for _ in range(2)]
        rstd = [sb(1, F32) for _ in range(2)]
        xn_bf = [sb(D, BF16) for _ in range(2)]
        cst = sb(5, F32)
        idn = sb(128, BF16)
        ptm = sb(128, BF16)
        bq = sb(4, F32); bk = sb(4, F32); bv = sb(4, F32); bg = sb(4, F32)
        gq = sb(16, F32); gk = sb(16, F32); gCc = sb(4, F32)
        mn128 = sb(1, F32); ss128 = sb(1, F32); sd128 = sb(1, F32); rs128 = sb(1, F32)
        scr128 = sb(128, F32)
        Sst = sb(4 * 128, F32)
        Stmp = sb(128, F32)
        Sbf = sb(4 * 128, BF16)
        shared_end = cur[0]

        xnT = sb(16 * C, BF16)
        wq_sb = sb(16 * 512, BF16)
        wk_sb = sb(16 * 512, BF16)
        wv_sb = sb(16 * 512, BF16)
        wg_sb = sb(16 * 512, BF16)
        wo_sb = sb(4 * D, BF16)
        ropeb = [sb(4 * C, F32)]
        decay = sb(4 * 896, BF16)
        q0 = sb(C, BF16); k0 = sb(C, BF16)
        qr = sb(C, BF16); kr = sb(C, BF16)
        tmp1 = sb(C, F32); tmp2 = sb(C, F32)
        scm = sb(4 * C, BF16)
        vT_sb = sb(C, BF16)
        v_a = sb(4 * 128, BF16)
        k_a = sb(4 * 128, BF16)
        u_s = sb(128, F32)
        rg = sb(128, F32)
        gn_bf = sb(128, BF16)
        gnT = sb(4 * C, BF16)
        gs = sb(C, F32)
        preT = sb(4 * C, BF16)
        pbuf = [sb(512, F32) for _ in range(2)]
        assert cur[0] <= ARENA

        cur[0] = shared_end
        QI = 4
        xnT_sp = sb(16 * C, BF16)
        fuse = sb(QI * C, BF16)
        ffn_acc = [sb(D, F32) for _ in range(4)]
        wdnb = [sb(D, BF16) for _ in range(QI)]
        wgt = [sb(16 * 128, BF16) for _ in range(2)]
        wut = [sb(16 * 128, BF16) for _ in range(2)]
        sg_sb = sb(64, F32); su_sb = sb(64, F32)
        sdbc = sb(D, F32); bobc = sb(D, F32)
        gs_b = [sb(C, F32) for _ in range(2)]
        gu_b = [sb(C, F32) for _ in range(2)]
        xo_b = [sb(D, F32)]
        ar_b = [sb(D, F32)]
        y_b = [sb(D, F32) for _ in range(2)]
        assert cur[0] <= ARENA, f"phase B overflow {cur[0]}"

        pP1 = E(nc.psum_tensor("pP1", [128, 512], F32))
        pP2 = E(nc.psum_tensor("pP2", [128, 512], F32))
        pS1 = E(nc.psum_tensor("pS1", [128, 512], F32))
        pS2 = E(nc.psum_tensor("pS2", [128, 512], F32))
        pM = E(nc.psum_tensor("pM", [128, 512], F32))
        pT = E(nc.psum_tensor("pT", [128, 512], BF16))
        pO1 = E(nc.psum_tensor("pO1", [128, 512], F32))
        pO2 = E(nc.psum_tensor("pO2", [128, 512], F32))

        semkeys = (["tS", "vS", "sS", "cc"]
                   + [f"dS{j}" for j in range(4)]
                   + [f"dG{j}" for j in range(4)]
                   + [f"dO{j}" for j in range(2)])
        sems = {k: E(nc.semaphore(k)) for k in semkeys}

        ops = {k: [] for k in "tvsgy"}
        cnt = {k: 0 for k in sems}
        waited = {}

        def W(eng, sk, val):
            if val is None or val <= 0:
                return
            if waited.get((eng, sk), 0) >= val:
                return
            waited[(eng, sk)] = val
            ops[eng].append(lambda e, sk=sk, v=val: e.wait_ge(sems[sk], v))

        def Wt(eng, tok):
            if tok is not None:
                W(eng, tok[0], tok[1])

        def I(eng, fn, sk=None, n=1):
            if sk:
                cnt[sk] += n

            def f(e, fn=fn, sk=sk, n=n):
                r = fn(e)
                if sk:
                    r.then_inc(sems[sk], n)
            ops[eng].append(f)
            return cnt[sk] if sk else None

        def mk_stream(qeng, prefix, depth):
            idx = [0]

            def issue(fn):
                j = idx[0] % depth
                sk = f"{prefix}{j}"
                if cnt[sk] > 0:
                    W(qeng, sk, cnt[sk])
                I(qeng, fn, sk, 16)
                idx[0] += 1
                return (sk, cnt[sk])
            return issue

        dS = mk_stream("y", "dS", 4)
        dG = mk_stream("g", "dG", 4)
        dO = mk_stream("y", "dO", 2)

        for t_, d_ in [(cst, cst_d), (idn, id_d), (ptm, pt_d),
                       (bq, bq_d), (bk, bk_d), (bv, bv_d), (bg, bg_d),
                       (gq, gq_d), (gk, gk_d), (gCc, gC_d)]:
            dS(lambda e, t_=t_, d_=d_: e.dma_start(out=t_[:], in_=d_))
        d_base = [(f"dS{j}", cnt[f"dS{j}"]) for j in range(4)]
        dG(lambda e: e.dma_start(
            out=decay[:].rearrange("p (h n) -> p h n", h=4),
            in_=decay_d.rearrange("p (h n) -> p h n", h=4)))
        for t_, d_ in [(wq_sb, wq_d), (wk_sb, wk_d), (wv_sb, wv_d), (wg_sb, wg_d)]:
            dG(lambda e, t_=t_, d_=d_: e.dma_start(
                out=t_[:].rearrange("p (c n) -> p c n", c=16),
                in_=d_.rearrange("(c p) n -> p c n", p=128)))
        dG(lambda e: e.dma_start(
            out=wo_sb[:].rearrange("p (h n) -> p h n", h=4),
            in_=wo_d.rearrange("(h p) n -> p h n", p=128)))
        dg_wproj = [(f"dG{j}", cnt[f"dG{j}"]) for j in range(4)]

        ln_done = {}
        ln_sq = {}
        xnbuf_free = {}
        rope_used = {}
        Sbf_done = [None] * 4
        pT_free = [None]
        po_free = [None, None]
        last_sc_evac = [None]

        def emit_ln(gt, src_dram, row0, slot, xnT_dst, col0):
            W("y", "vS", ln_done.get(gt - 2))
            W("y", "sS", ln_sq.get(gt - 2))
            dv = dS(lambda e, src_dram=src_dram, row0=row0, slot=slot:
                    e.dma_start(out=xbuf[slot][:],
                                in_=src_dram[row0:row0 + 128, :]))
            Wt("v", dv)
            I("v", lambda e, slot=slot: e.tensor_reduce(
                msum[slot][:], xbuf[slot][:], AX.X, A.add), "vS")
            W("s", "vS", cnt["vS"])
            for tok in d_base:
                Wt("s", tok)
            I("s", lambda e, slot=slot: e.activation(
                mneg[slot][:], msum[slot][:], AF.Identity, scale=cst[:, 0:1]), "sS")
            W("s", "sS", cnt["sS"])
            I("s", lambda e, slot=slot: e.activation(
                scr[:], xbuf[slot][:], AF.Square, bias=mneg[slot][:],
                accum_out=sdv[slot][:]), "sS")
            ln_sq[gt] = cnt["sS"]
            W("s", "sS", cnt["sS"])
            I("s", lambda e, slot=slot: e.activation(
                msum[slot][:], sdv[slot][:], AF.Sqrt, scale=cst[:, 1:2],
                bias=cst[:, 4:5]), "sS")
            W("v", "sS", cnt["sS"])
            I("v", lambda e, slot=slot: e.reciprocal(rstd[slot][:], msum[slot][:]),
              "vS")
            W("v", "vS", cnt["vS"])
            W("v", "tS", xnbuf_free.get(gt - 2))
            lv = I("v", lambda e, slot=slot: e.tensor_scalar(
                xn_bf[slot][:], xbuf[slot][:], mneg[slot][:], rstd[slot][:],
                A.add, A.mult), "vS")
            ln_done[gt] = lv
            for grp in range(4):
                for tok in d_base:
                    Wt("t", tok)
                W("t", "vS", lv)
                W("t", "vS", pT_free[0])
                for j in range(4):
                    jj = grp * 4 + j
                    I("t", lambda e, jj=jj, j=j, slot=slot: e.transpose(
                        pT[:, j * 128:(j + 1) * 128],
                        xn_bf[slot][:, jj * 128:(jj + 1) * 128], idn[:]),
                      "tS" if j == 3 else None)
                W("v", "tS", cnt["tS"])
                pT_free[0] = I("v", lambda e, grp=grp, xnT_dst=xnT_dst, col0=col0:
                               e.tensor_copy(
                    xnT_dst[:].rearrange("p (c n) -> p c n", c=16)
                    [:, grp * 4:grp * 4 + 4, col0:col0 + 128],
                    pT[:].rearrange("p (c n) -> p c n", c=4)), "vS")
            xnbuf_free[gt] = cnt["tS"]

        partial_t_prev = [None]
        last_supd = [None]
        for ch in range(NCHD):
            W("y", "vS", rope_used.get(ch - 1))
            dr = dS(lambda e, ch=ch: e.dma_start(
                out=ropeb[0][:].rearrange("p (t n) -> p t n", t=4),
                in_=rope_d.rearrange("p (t n) -> p t n", t=4)
                [:, :, ch * C:(ch + 1) * C]))
            for ti in range(NT):
                gt = ch * NT + ti
                emit_ln(gt, x_full, gt * 128, gt % 2, xnT, ti * 128)
            xnT_done = cnt["vS"]

            rv = ropeb[0][:].rearrange("p (t n) -> p t n", t=4)
            for h in range(4):
                for tok in dg_wproj:
                    Wt("t", tok)
                W("t", "vS", xnT_done)
                W("t", "sS", last_sc_evac[0])
                for kc in range(16):
                    I("t", lambda e, kc=kc, h=h: e.matmul(
                        pP1[:], wq_sb[:, kc * 512 + h * 128: kc * 512 + (h + 1) * 128],
                        xnT[:, kc * C:(kc + 1) * C],
                        start=(kc == 0), stop=(kc == 15)), "tS" if kc == 15 else None)
                W("s", "tS", cnt["tS"])
                eq = I("s", lambda e, h=h: e.activation(
                    q0[:], pP1[:], AF.Identity, bias=bq[:, h:h + 1]), "sS")
                for kc in range(16):
                    I("t", lambda e, kc=kc, h=h: e.matmul(
                        pP2[:], wk_sb[:, kc * 512 + h * 128: kc * 512 + (h + 1) * 128],
                        xnT[:, kc * C:(kc + 1) * C],
                        start=(kc == 0), stop=(kc == 15)), "tS" if kc == 15 else None)
                W("s", "tS", cnt["tS"])
                ek = I("s", lambda e, h=h: e.activation(
                    k0[:], pP2[:], AF.Identity, bias=bk[:, h:h + 1]), "sS")
                W("t", "sS", eq)
                I("t", lambda e: e.matmul(pP1[:], ptm[:], q0[:],
                                          start=True, stop=True), "tS")
                trq = cnt["tS"]
                W("t", "sS", ek)
                I("t", lambda e: e.matmul(pP2[:], ptm[:], k0[:],
                                          start=True, stop=True), "tS")
                trk = cnt["tS"]
                Wt("v", dr)
                W("v", "tS", trq)
                I("v", lambda e: e.tensor_tensor(tmp1[:], q0[:], rv[:, 0, :],
                                                 A.mult), "vS")
                I("v", lambda e: e.tensor_tensor(tmp2[:], pP1[:], rv[:, 1, :],
                                                 A.mult), "vS")
                W("v", "vS", cnt["vS"])
                vq = I("v", lambda e: e.tensor_tensor(qr[:], tmp1[:], tmp2[:],
                                                      A.add), "vS")
                W("v", "tS", trk)
                W("v", "vS", cnt["vS"])
                I("v", lambda e: e.tensor_tensor(tmp1[:], k0[:], rv[:, 2, :],
                                                 A.mult), "vS")
                I("v", lambda e: e.tensor_tensor(tmp2[:], pP2[:], rv[:, 3, :],
                                                 A.mult), "vS")
                W("v", "vS", cnt["vS"])
                vk = I("v", lambda e: e.tensor_tensor(kr[:], tmp1[:], tmp2[:],
                                                      A.add), "vS")
                rope_used[ch] = vk

                W("t", "vS", vq)
                for kc in range(16):
                    I("t", lambda e, kc=kc, h=h: e.matmul(
                        pP1[:], wv_sb[:, kc * 512 + h * 128: kc * 512 + (h + 1) * 128],
                        xnT[:, kc * C:(kc + 1) * C],
                        start=(kc == 0), stop=(kc == 15)), "tS" if kc == 15 else None)
                W("s", "tS", cnt["tS"])
                ev = I("s", lambda e, h=h: e.activation(
                    vT_sb[:], pP1[:], AF.Identity, bias=bv[:, h:h + 1]), "sS")
                W("t", "sS", ev)
                W("t", "vS", pT_free[0])
                for kt in range(4):
                    I("t", lambda e, kt=kt: e.transpose(
                        pT[:, kt * 128:(kt + 1) * 128],
                        vT_sb[:, kt * 128:(kt + 1) * 128], idn[:]),
                      "tS" if kt == 3 else None)
                W("v", "tS", cnt["tS"])
                pT_free[0] = I("v", lambda e: e.tensor_copy(v_a[:], pT[:]), "vS")
                va = pT_free[0]

                W("t", "vS", vk)
                msk_done = [None, None]
                for kt in range(4):
                    psum = pS1 if kt % 2 == 0 else pS2
                    W("t", "vS", msk_done[kt % 2])
                    I("t", lambda e, kt=kt, psum=psum: e.matmul(
                        psum[:], kr[:, kt * 128:(kt + 1) * 128], qr[:],
                        start=True, stop=True), "tS")
                    W("v", "tS", cnt["tS"])
                    off = 384 - 128 * kt
                    mv = I("v", lambda e, kt=kt, psum=psum, off=off, h=h:
                           e.tensor_tensor(
                        scm[:, kt * C:(kt + 1) * C], psum[:],
                        decay[:].rearrange("p (h n) -> p h n", h=4)
                        [:, h, off:off + 512], A.mult), "vS")
                    msk_done[kt % 2] = mv
                scm_done = cnt["vS"]

                W("t", "vS", va)
                for kt in range(4):
                    I("t", lambda e, kt=kt: e.transpose(
                        pT[:, kt * 128:(kt + 1) * 128],
                        kr[:, kt * 128:(kt + 1) * 128], idn[:]),
                      "tS" if kt == 3 else None)
                W("s", "tS", cnt["tS"])
                for kt in range(4):
                    I("s", lambda e, kt=kt, h=h: e.activation(
                        k_a[:, kt * 128:(kt + 1) * 128],
                        pT[:, kt * 128:(kt + 1) * 128],
                        AF.Identity, scale=gk[:, h * 4 + kt:h * 4 + kt + 1]), "sS")
                ka_done = cnt["sS"]
                pT_free[0] = None
                ka_sS = ka_done

                gn_last = None
                for tt in range(4):
                    if ch > 0 and not DBG_NO_U:
                        W("t", "vS", Sbf_done[h])
                        I("t", lambda e, tt=tt, h=h: e.matmul(
                            pS1[:, 0:128], qr[:, tt * 128:(tt + 1) * 128],
                            Sbf[:, h * 128:(h + 1) * 128],
                            start=True, stop=True), "tS")
                        W("v", "tS", cnt["tS"])
                        eu = I("v", lambda e, tt=tt, h=h: e.tensor_scalar(
                            u_s[:], pS1[:, 0:128],
                            gq[:, h * 4 + tt:h * 4 + tt + 1], None,
                            A.mult, A.bypass), "vS")
                    W("t", "vS", scm_done if tt == 0 else None)
                    for kt in range(tt + 1):
                        I("t", lambda e, kt=kt, tt=tt: e.matmul(
                            pM[:, 128:256],
                            scm[:, kt * C + tt * 128: kt * C + (tt + 1) * 128],
                            v_a[:, kt * 128:(kt + 1) * 128],
                            start=(kt == 0), stop=(kt == tt)),
                          "tS" if kt == tt else None)
                    W("v", "tS", cnt["tS"])
                    if ch > 0 and not DBG_NO_U:
                        W("v", "vS", eu)
                        I("v", lambda e: e.tensor_tensor(
                            rg[:], pM[:, 128:256], u_s[:], A.add), "vS")
                    else:
                        I("v", lambda e: e.tensor_copy(rg[:], pM[:, 128:256]), "vS")
                    W("v", "vS", cnt["vS"])
                    I("v", lambda e: e.tensor_reduce(mn128[:], rg[:], AX.X, A.add),
                      "vS")
                    W("s", "vS", cnt["vS"])
                    I("s", lambda e: e.activation(
                        mn128[:], mn128[:], AF.Identity, scale=cst[:, 2:3]), "sS")
                    W("s", "sS", cnt["sS"])
                    I("s", lambda e: e.activation(
                        scr128[:], rg[:], AF.Square, bias=mn128[:],
                        accum_out=ss128[:]), "sS")
                    W("s", "sS", cnt["sS"])
                    I("s", lambda e: e.activation(
                        sd128[:], ss128[:], AF.Sqrt, scale=cst[:, 3:4],
                        bias=cst[:, 4:5]), "sS")
                    W("v", "sS", cnt["sS"])
                    I("v", lambda e: e.reciprocal(rs128[:], sd128[:]), "vS")
                    W("v", "vS", cnt["vS"])
                    gv = I("v", lambda e: e.tensor_scalar(
                        gn_bf[:], rg[:], mn128[:], rs128[:], A.add, A.mult), "vS")
                    W("t", "vS", gv)
                    W("t", "sS", ka_sS)
                    I("t", lambda e: e.transpose(pT[:, 0:128], gn_bf[:], idn[:]),
                      "tS")
                    W("v", "tS", cnt["tS"])
                    gn_last = I("v", lambda e, h=h, tt=tt: e.tensor_copy(
                        gnT[:, h * C + tt * 128: h * C + (tt + 1) * 128],
                        pT[:, 0:128]), "vS")
                pT_free[0] = gn_last

                W("t", "sS", ka_done)
                W("t", "vS", last_supd[0])
                for kt in range(4):
                    I("t", lambda e, kt=kt: e.matmul(
                        pM[:, 256:384], k_a[:, kt * 128:(kt + 1) * 128],
                        v_a[:, kt * 128:(kt + 1) * 128],
                        start=(kt == 0), stop=(kt == 3)), "tS" if kt == 3 else None)
                st_mm = cnt["tS"]
                if ch > 0 and not DBG_NO_SUPD:
                    W("s", "vS", last_supd[0])
                    es = I("s", lambda e, h=h: e.activation(
                        Stmp[:], Sst[:, h * 128:(h + 1) * 128], AF.Identity,
                        scale=gCc[:, h:h + 1]), "sS")
                    W("v", "sS", es)
                    W("v", "tS", st_mm)
                    I("v", lambda e, h=h: e.tensor_tensor(
                        Sst[:, h * 128:(h + 1) * 128], Stmp[:], pM[:, 256:384],
                        A.add), "vS")
                else:
                    W("v", "tS", st_mm)
                    I("v", lambda e, h=h: e.tensor_copy(
                        Sst[:, h * 128:(h + 1) * 128], pM[:, 256:384]), "vS")
                last_supd[0] = cnt["vS"]
                W("v", "vS", cnt["vS"])
                Sbf_done[h] = I("v", lambda e, h=h: e.tensor_copy(
                    Sbf[:, h * 128:(h + 1) * 128],
                    Sst[:, h * 128:(h + 1) * 128]), "vS")

                for kc in range(16):
                    I("t", lambda e, kc=kc, h=h: e.matmul(
                        pP2[:], wg_sb[:, kc * 512 + h * 128: kc * 512 + (h + 1) * 128],
                        xnT[:, kc * C:(kc + 1) * C],
                        start=(kc == 0), stop=(kc == 15)), "tS" if kc == 15 else None)
                W("s", "tS", cnt["tS"])
                eg = I("s", lambda e, h=h: e.activation(
                    gs[:], pP2[:], AF.Silu, bias=bg[:, h:h + 1]), "sS")
                last_sc_evac[0] = eg
                W("v", "sS", eg)
                W("v", "tS", partial_t_prev[0])
                I("v", lambda e, h=h: e.tensor_tensor(
                    preT[:, h * C:(h + 1) * C], gs[:], gnT[:, h * C:(h + 1) * C],
                    A.mult), "vS")
            preT_done = cnt["vS"]

            W("t", "vS", preT_done)
            pbuf_free = [None, None]
            for tt in range(4):
                for dc in range(4):
                    idx = tt * 4 + dc
                    psum = pO1 if idx % 2 == 0 else pO2
                    W("t", "vS", po_free[idx % 2])
                    for h in range(4):
                        I("t", lambda e, h=h, tt=tt, dc=dc, psum=psum: e.matmul(
                            psum[:], preT[:, h * C + tt * 128: h * C + (tt + 1) * 128],
                            wo_sb[:, h * D + dc * 512: h * D + (dc + 1) * 512],
                            start=(h == 0), stop=(h == 3)), "tS" if h == 3 else None)
                    W("v", "tS", cnt["tS"])
                    slot = idx % 2
                    Wt("v", pbuf_free[slot])
                    pv = I("v", lambda e, psum=psum, slot=slot: e.tensor_copy(
                        pbuf[slot][:], psum[:]), "vS")
                    po_free[idx % 2] = pv
                    W("y", "vS", pv)
                    r0 = ch * C + tt * 128
                    pbuf_free[slot] = dS(lambda e, slot=slot, r0=r0, dc=dc:
                                         e.dma_start(
                        out=cc_in[r0:r0 + 128, dc * 512:(dc + 1) * 512],
                        in_=pbuf[slot][:]))
            partial_t_prev[0] = cnt["tS"]

        for j in range(4):
            W("g", f"dS{j}", cnt[f"dS{j}"])
        I("g", lambda e: e.collective_compute(
            "ReduceScatter", A.add,
            replica_groups=[[0, 1, 2, 3], [4, 5, 6, 7]],
            ins=[cc_in_t.ap().opt()],
            outs=[cc_out_t.ap().opt()]), "cc")

        db_scales = [dS(lambda e, t_=t_, d_=d_: e.dma_start(out=t_[:], in_=d_))
                     for t_, d_ in [(sg_sb, sg_d), (su_sb, su_d),
                                    (sdbc, sdbc_d), (bobc, bobc_d)]]

        wgv = wgate_d.rearrange("(c p) n -> p c n", p=128)
        wuv = wup_d.rearrange("(c p) n -> p c n", p=128)
        wt_free = [None, None]
        wdn_free = [None] * QI
        fuse_free = [None]
        gs_free = [None, None]
        psum_free = [None, None]
        out_dmas = [0]
        ar_done = [None]
        yout_tok = [None, None]
        acc_wm = [0]
        NQ = 64 // QI

        for sp in range(NSPD):
            for ti in range(NT):
                emit_ln(200 + sp * NT + ti, x_own, sp * C + ti * 128, ti % 2,
                        xnT_sp, ti * 128)
            xnsp_done = cnt["vS"]
            for qb in range(min(NQ, NQD)):
                dg3 = None
                for icl in range(QI):
                    ic = qb * QI + icl
                    slot = ic % 2
                    pG = pP1 if ic % 2 == 0 else pS1
                    pU = pP2 if ic % 2 == 0 else pS2
                    W("g", "tS", wt_free[slot])
                    dG(lambda e, ic=ic, slot=slot: e.dma_start(
                        out=wgt[slot][:].rearrange("p (c n) -> p c n", c=16),
                        in_=wgv[:, :, ic * 128:(ic + 1) * 128]))
                    dg2 = dG(lambda e, ic=ic, slot=slot: e.dma_start(
                        out=wut[slot][:].rearrange("p (c n) -> p c n", c=16),
                        in_=wuv[:, :, ic * 128:(ic + 1) * 128]))
                    W("g", "tS", wdn_free[icl])
                    dg3 = dG(lambda e, ic=ic, icl=icl: e.dma_start(
                        out=wdnb[icl][:],
                        in_=wdn_d[ic * 128:(ic + 1) * 128, :]))
                    Wt("t", dg2)
                    W("t", "vS", xnsp_done)
                    W("t", "sS", psum_free[ic % 2])
                    for kc in range(16):
                        I("t", lambda e, kc=kc, slot=slot, pG=pG: e.matmul(
                            pG[:], wgt[slot][:, kc * 128:(kc + 1) * 128],
                            xnT_sp[:, kc * C:(kc + 1) * C],
                            start=(kc == 0), stop=(kc == 15)),
                          "tS" if kc == 15 else None)
                    tg = cnt["tS"]
                    for kc in range(16):
                        I("t", lambda e, kc=kc, slot=slot, pU=pU: e.matmul(
                            pU[:], wut[slot][:, kc * 128:(kc + 1) * 128],
                            xnT_sp[:, kc * C:(kc + 1) * C],
                            start=(kc == 0), stop=(kc == 15)),
                          "tS" if kc == 15 else None)
                    wt_free[slot] = cnt["tS"]
                    W("s", "tS", tg)
                    for tok in db_scales:
                        Wt("s", tok)
                    W("s", "vS", gs_free[slot])
                    I("s", lambda e, ic=ic, slot=slot, pG=pG: e.activation(
                        gs_b[slot][:], pG[:], AF.Silu,
                        scale=sg_sb[:, ic:ic + 1]), "sS")
                    W("s", "tS", wt_free[slot])
                    psum_free[ic % 2] = I(
                        "s", lambda e, ic=ic, slot=slot, pU=pU: e.activation(
                            gu_b[slot][:], pU[:], AF.Identity,
                            scale=su_sb[:, ic:ic + 1]), "sS")
                    W("v", "sS", psum_free[ic % 2])
                    W("v", "tS", fuse_free[0])
                    gs_free[slot] = I("v", lambda e, icl=icl, slot=slot:
                                      e.tensor_tensor(
                        fuse[:].rearrange("p (c n) -> p c n", c=QI)[:, icl, :],
                        gs_b[slot][:], gu_b[slot][:], A.mult), "vS")
                fuse_done = cnt["vS"]
                W("v", "vS", acc_wm[0])
                W("t", "vS", fuse_done)
                Wt("t", dg3)
                for tt in range(4):
                    for dc in range(4):
                        idx = tt * 4 + dc
                        psum = pO1 if idx % 2 == 0 else pO2
                        W("t", "vS", po_free[idx % 2])
                        for icl in range(QI):
                            I("t", lambda e, icl=icl, tt=tt, dc=dc, psum=psum:
                              e.matmul(
                                psum[:],
                                fuse[:].rearrange("p (c n) -> p c n", c=QI)
                                [:, icl, tt * 128:(tt + 1) * 128],
                                wdnb[icl][:, dc * 512:(dc + 1) * 512],
                                start=(icl == 0), stop=(icl == QI - 1)),
                              "tS" if icl == QI - 1 else None)
                        W("v", "tS", cnt["tS"])
                        if qb == 0:
                            po_free[idx % 2] = I(
                                "v", lambda e, tt=tt, dc=dc, psum=psum:
                                e.tensor_copy(
                                    ffn_acc[tt][:, dc * 512:(dc + 1) * 512],
                                    psum[:]), "vS")
                        else:
                            po_free[idx % 2] = I(
                                "v", lambda e, tt=tt, dc=dc, psum=psum:
                                e.tensor_tensor(
                                    ffn_acc[tt][:, dc * 512:(dc + 1) * 512],
                                    ffn_acc[tt][:, dc * 512:(dc + 1) * 512],
                                    psum[:], A.add), "vS")
                for w in range(QI):
                    wdn_free[w] = cnt["tS"]
                fuse_free[0] = cnt["tS"]
                acc_wm[0] = cnt["vS"]
            for tt in range(4):
                slot = tt % 2
                W("y", "cc", 1)
                W("y", "vS", ar_done[0])
                r0 = sp * C + tt * 128
                da = dS(lambda e, r0=r0: e.dma_start(
                    out=ar_b[0][:], in_=cc_out[r0:r0 + 128, :]))
                dx = dS(lambda e, r0=r0: e.dma_start(
                    out=xo_b[0][:], in_=x_own[r0:r0 + 128, :]))
                Wt("v", da)
                Wt("v", dx)
                for tok in db_scales:
                    Wt("v", tok)
                Wt("v", yout_tok[slot])
                W("v", "vS", acc_wm[0])
                I("v", lambda e, tt=tt, slot=slot: e.tensor_tensor(
                    y_b[slot][:], ffn_acc[tt][:], sdbc[:], A.mult), "vS")
                W("v", "vS", cnt["vS"])
                I("v", lambda e, slot=slot: e.tensor_tensor(
                    y_b[slot][:], y_b[slot][:], bobc[:], A.add), "vS")
                W("v", "vS", cnt["vS"])
                I("v", lambda e, slot=slot: e.tensor_tensor(
                    y_b[slot][:], y_b[slot][:], xo_b[0][:], A.add), "vS")
                W("v", "vS", cnt["vS"])
                yv = I("v", lambda e, slot=slot: e.tensor_tensor(
                    y_b[slot][:], y_b[slot][:], ar_b[0][:], A.add), "vS")
                ar_done[0] = yv
                W("y", "vS", yv)
                yout_tok[slot] = dO(lambda e, slot=slot, r0=r0: e.dma_start(
                    out=y_d[r0:r0 + 128, :], in_=y_b[slot][:]))
                out_dmas[0] += 1

        for j in range(2):
            ops["y"].append(lambda e, j=j: e.wait_ge(sems[f"dO{j}"], cnt[f"dO{j}"]))

        with nc.Block() as block:
            @block.tensor
            def _(t):
                for f in ops["t"]:
                    f(t)

            @block.vector
            def _(v):
                for f in ops["v"]:
                    f(v)

            @block.scalar
            def _(s):
                for f in ops["s"]:
                    f(s)

            @block.gpsimd
            def _(g):
                for f in ops["g"]:
                    f(g)

            @block.sync
            def _(y):
                for f in ops["y"]:
                    f(y)

    return nc


def host_tables():
    pos = np.arange(S, dtype=np.float64)
    invf = 1.0 / THETA ** (np.arange(0, DK, 2, dtype=np.float64) / DK)
    j = np.arange(DK) // 2
    ang = pos[None, :] * invf[j][:, None]
    cos, sin = np.cos(ang), np.sin(ang)
    sbase = (np.arange(0, DK, 2, dtype=np.float64) + 0.4 * DK) / (1.4 * DK)
    scale = sbase[j][:, None] ** ((pos - S // 2) / SBASE)[None, :]
    ropes = np.concatenate([cos * scale, sin * scale, cos / scale, sin / scale],
                           axis=1).astype(np.float32)
    P = np.zeros((DK, DK), np.float64)
    for jj in range(0, DK, 2):
        P[jj, jj + 1] = -1.0
        P[jj + 1, jj] = 1.0
    PT = np.ascontiguousarray(P.T).astype(ml_dtypes.bfloat16)
    ident = np.eye(128, dtype=np.float32).astype(ml_dtypes.bfloat16)
    consts = np.ascontiguousarray(np.broadcast_to(np.array(
        [-1.0 / D, 1.0 / D, -1.0 / DK, 1.0 / DK, EPS], np.float32), (128, 5)))
    return ropes, PT, ident, consts


def decay_tables(heads):
    T4, gq4, gk4, gC4 = [], [], [], []
    p = np.arange(128, dtype=np.float64)
    for h in heads:
        g = 1.0 - 2.0 ** (-5.0 - float(h))
        d = np.arange(896, dtype=np.float64)[None, :] - 384.0 - p[:, None]
        T4.append(np.where(d >= 0, g ** np.maximum(d, 0.0), 0.0))
        gq4.append(np.stack([g ** (128.0 * t + p + 1.0) for t in range(4)], 1))
        gk4.append(np.stack([g ** (511.0 - 128.0 * kt - p) for kt in range(4)], 1))
        gC4.append(np.full(128, g ** C))
    return (np.concatenate(T4, 1).astype(np.float32),
            np.concatenate(gq4, 1).astype(np.float32),
            np.concatenate(gk4, 1).astype(np.float32),
            np.stack(gC4, 1).astype(np.float32))


_NC = None


def kernel(L0_attn_x, L0_attn_Wq, L0_attn_bq, L0_attn_Wk, L0_attn_bk,
           L0_attn_Wv, L0_attn_bv, L0_attn_Wg, L0_attn_bg, L0_attn_Wo,
           L0_attn_bo, L0_ffn_w_gate, L0_ffn_w_up, L0_ffn_w_down,
           L0_ffn_s_gate, L0_ffn_s_up, L0_ffn_s_down):
    global _NC
    if _NC is None:
        _NC = build_nc()
    nc = _NC
    ropes, PT, ident, consts = host_tables()
    x = np.ascontiguousarray(np.asarray(L0_attn_x, np.float32))
    f32 = lambda a: np.ascontiguousarray(np.asarray(a, np.float32))
    i8 = lambda a: np.ascontiguousarray(np.asarray(a, np.int8))
    col4 = lambda a, hg: np.ascontiguousarray(
        f32(a)[4 * hg * 128:(4 * hg + 4) * 128].reshape(4, 128).T)
    sg_c = np.ascontiguousarray(f32(L0_ffn_s_gate).reshape(64, 128).T)
    su_c = np.ascontiguousarray(f32(L0_ffn_s_up).reshape(64, 128).T)
    sd_bc = np.ascontiguousarray(np.broadcast_to(f32(L0_ffn_s_down), (128, D)))
    bo_bc = np.ascontiguousarray(np.broadcast_to(f32(L0_attn_bo), (128, D)))
    wg_full = i8(L0_ffn_w_gate)
    wu_full = i8(L0_ffn_w_up)
    wd_full = i8(L0_ffn_w_down)
    in_maps = []
    for c in range(8):
        b, hg, qt = c // 4, c % 4, c % 4
        heads = list(range(4 * hg, 4 * hg + 4))
        decayT, gq_c, gk_c, gC_c = decay_tables(heads)
        sl = slice(4 * hg * 128, (4 * hg + 4) * 128)
        in_maps.append(dict(
            x_full=x[b],
            x_own=np.ascontiguousarray(x[b, qt * KB:(qt + 1) * KB]),
            wq_s=np.ascontiguousarray(f32(L0_attn_Wq)[:, sl]),
            wk_s=np.ascontiguousarray(f32(L0_attn_Wk)[:, sl]),
            wv_s=np.ascontiguousarray(f32(L0_attn_Wv)[:, sl]),
            wg_s=np.ascontiguousarray(f32(L0_attn_Wg)[:, sl]),
            wo_s=np.ascontiguousarray(f32(L0_attn_Wo)[sl, :]),
            bq_c=col4(L0_attn_bq, hg), bk_c=col4(L0_attn_bk, hg),
            bv_c=col4(L0_attn_bv, hg), bg_c=col4(L0_attn_bg, hg),
            w_gate=wg_full, w_up=wu_full, w_down=wd_full,
            sg_c=sg_c, su_c=su_c, sd_bc=sd_bc, bo_bc=bo_bc,
            ropes=ropes, decayT=decayT, gq_c=gq_c, gk_c=gk_c, gC_c=gC_c,
            consts=consts, ident=ident, PT=PT,
        ))
    res = run_bass_kernel_spmd(nc, in_maps, core_ids=list(range(8)))
    out = np.empty((B, S, D), np.float32)
    for c in range(8):
        out[c // 4, (c % 4) * KB:(c % 4 + 1) * KB] = res.results[c]["y"]
    return out


# version 02
# speedup vs baseline: 1.0558x; 1.0558x over previous
import numpy as np
import ml_dtypes
import concourse.bass as bass
import concourse.mybir as mybir
from concourse.bass_utils import run_bass_kernel_spmd
from contextlib import ExitStack

F32 = mybir.dt.float32
BF16 = mybir.dt.bfloat16
I8 = mybir.dt.int8
A = mybir.AluOpType
AF = mybir.ActivationFunctionType
AX = mybir.AxisListType

B, S, D, H, DK, INTER = 2, 4096, 2048, 16, 128, 8192
C = 512
NCH = S // C
NT = C // 128
EPS = 1e-5
THETA = 1e4
SBASE = 512.0
KB = 1024
NCHD = NCH
NQD = 16
NSPD = 2
DBG_NO_U = False
DBG_U_LEVEL = 3
DBG_NO_SUPD = False


def build_nc():
    nc = bass.Bass("TRN2", target_bir_lowering=False, debug=False)

    def din(name, shape, dt):
        return nc.dram_tensor(name, shape, dt, kind="ExternalInput").ap()

    x_full = din("x_full", [S, D], F32)
    x_own = din("x_own", [KB, D], F32)
    wq_d = din("wq_s", [D, 512], F32)
    wk_d = din("wk_s", [D, 512], F32)
    wv_d = din("wv_s", [D, 512], F32)
    wg_d = din("wg_s", [D, 512], F32)
    wo_d = din("wo_s", [512, D], F32)
    bq_d = din("bq_c", [128, 4], F32)
    bk_d = din("bk_c", [128, 4], F32)
    bv_d = din("bv_c", [128, 4], F32)
    bg_d = din("bg_c", [128, 4], F32)
    wgate_d = din("w_gate", [D, INTER], I8)
    wup_d = din("w_up", [D, INTER], I8)
    wdn_d = din("w_down", [INTER, D], I8)
    sg_d = din("sg_c", [128, 64], F32)
    su_d = din("su_c", [128, 64], F32)
    sdbc_d = din("sd_bc", [128, D], F32)
    bobc_d = din("bo_bc", [128, D], F32)
    rope_d = din("ropes", [128, 4 * S], F32)
    decay_d = din("decayT", [128, 4 * 896], F32)
    gq_d = din("gq_c", [128, 16], F32)
    gk_d = din("gk_c", [128, 16], F32)
    gC_d = din("gC_c", [128, 4], F32)
    cst_d = din("consts", [128, 5], F32)
    id_d = din("ident", [128, 128], BF16)
    pt_d = din("PT", [128, 128], BF16)

    y_d = nc.dram_tensor("y", [KB, D], F32, kind="ExternalOutput").ap()
    cc_in_t = nc.dram_tensor("cc_in", [S, D], BF16)
    cc_out_t = nc.dram_tensor("cc_out", [KB, D], BF16)
    cc_in = cc_in_t.ap()
    cc_out = cc_out_t.ap()

    with ExitStack() as st:
        E = st.enter_context
        ARENA = 190 * KB
        arena = E(nc.sbuf_tensor("arena", [128, ARENA], I8))
        cur = [0]

        def sb(cols, dt):
            nb = cols * {F32: 4, BF16: 2, I8: 1}[dt]
            nb = (nb + 511) // 512 * 512
            off = cur[0]
            cur[0] += nb
            assert cur[0] <= ARENA, f"arena overflow {cur[0]}"
            return arena[:, off:off + cols * {F32: 4, BF16: 2, I8: 1}[dt]].bitcast(dt)

        xbuf = [sb(D, F32) for _ in range(2)]
        scr = sb(D, F32)
        msum = [sb(1, F32) for _ in range(2)]
        mneg = [sb(1, F32) for _ in range(2)]
        sdv = [sb(1, F32) for _ in range(2)]
        rstd = [sb(1, F32) for _ in range(2)]
        xn_bf = [sb(D, BF16) for _ in range(2)]
        cst = sb(5, F32)
        idn = sb(128, BF16)
        ptm = sb(128, BF16)
        bq = sb(4, F32); bk = sb(4, F32); bv = sb(4, F32); bg = sb(4, F32)
        gq = sb(16, F32); gk = sb(16, F32); gCc = sb(4, F32)
        mn128 = sb(1, F32); ss128 = sb(1, F32); sd128 = sb(1, F32); rs128 = sb(1, F32)
        scr128 = sb(128, F32)
        Sst = sb(4 * 128, F32)
        Stmp = sb(128, F32)
        Sbf = sb(4 * 128, BF16)
        shared_end = cur[0]

        xnT = sb(16 * C, BF16)
        wq_sb = sb(16 * 512, BF16)
        wk_sb = sb(16 * 512, BF16)
        wv_sb = sb(16 * 512, BF16)
        wg_sb = sb(16 * 512, BF16)
        wo_sb = sb(4 * D, BF16)
        ropeb = [sb(4 * C, F32)]
        decay = sb(4 * 896, BF16)
        q0 = sb(C, BF16); k0 = sb(C, BF16)
        qr = sb(C, BF16); kr = sb(C, BF16)
        tmp1 = sb(C, F32); tmp2 = sb(C, F32)
        scm = sb(4 * C, BF16)
        vT_sb = sb(C, BF16)
        v_a = sb(4 * 128, BF16)
        k_a = sb(4 * 128, BF16)
        u_s = sb(128, F32)
        rg = sb(128, F32)
        gn_bf = sb(128, BF16)
        gnT = sb(4 * C, BF16)
        gs = sb(C, F32)
        preT = sb(4 * C, BF16)
        pbuf = [sb(512, BF16) for _ in range(2)]
        assert cur[0] <= ARENA

        cur[0] = shared_end
        QI = 4
        xnT_sp = sb(16 * C, BF16)
        fuse = sb(QI * C, BF16)
        ffn_acc = [sb(D, F32) for _ in range(4)]
        wdnb = [sb(D, BF16) for _ in range(QI)]
        wgt = [sb(16 * 128, BF16) for _ in range(2)]
        wut = [sb(16 * 128, BF16) for _ in range(2)]
        sg_sb = sb(64, F32); su_sb = sb(64, F32)
        sdbc = sb(D, F32); bobc = sb(D, F32)
        gs_b = [sb(C, F32) for _ in range(2)]
        gu_b = [sb(C, F32) for _ in range(2)]
        xo_b = [sb(D, F32)]
        ar_b = [sb(D, BF16)]
        y_b = [sb(D, F32) for _ in range(2)]
        assert cur[0] <= ARENA, f"phase B overflow {cur[0]}"

        pP1 = E(nc.psum_tensor("pP1", [128, 512], F32))
        pP2 = E(nc.psum_tensor("pP2", [128, 512], F32))
        pS1 = E(nc.psum_tensor("pS1", [128, 512], F32))
        pS2 = E(nc.psum_tensor("pS2", [128, 512], F32))
        pM = E(nc.psum_tensor("pM", [128, 512], F32))
        pT = E(nc.psum_tensor("pT", [128, 512], BF16))
        pO1 = E(nc.psum_tensor("pO1", [128, 512], F32))
        pO2 = E(nc.psum_tensor("pO2", [128, 512], F32))

        semkeys = (["tS", "vS", "sS", "cc"]
                   + [f"dS{j}" for j in range(4)]
                   + [f"dG{j}" for j in range(4)]
                   + [f"dO{j}" for j in range(2)])
        sems = {k: E(nc.semaphore(k)) for k in semkeys}

        ops = {k: [] for k in "tvsgy"}
        cnt = {k: 0 for k in sems}
        waited = {}

        def W(eng, sk, val):
            if val is None or val <= 0:
                return
            if waited.get((eng, sk), 0) >= val:
                return
            waited[(eng, sk)] = val
            ops[eng].append(lambda e, sk=sk, v=val: e.wait_ge(sems[sk], v))

        def Wt(eng, tok):
            if tok is not None:
                W(eng, tok[0], tok[1])

        def I(eng, fn, sk=None, n=1):
            if sk:
                cnt[sk] += n

            def f(e, fn=fn, sk=sk, n=n):
                r = fn(e)
                if sk:
                    r.then_inc(sems[sk], n)
            ops[eng].append(f)
            return cnt[sk] if sk else None

        def mk_stream(qeng, prefix, depth):
            idx = [0]

            def issue(fn):
                j = idx[0] % depth
                sk = f"{prefix}{j}"
                if cnt[sk] > 0:
                    W(qeng, sk, cnt[sk])
                I(qeng, fn, sk, 16)
                idx[0] += 1
                return (sk, cnt[sk])
            return issue

        dS = mk_stream("y", "dS", 4)
        dG = mk_stream("g", "dG", 4)
        dO = mk_stream("y", "dO", 2)

        for t_, d_ in [(cst, cst_d), (idn, id_d), (ptm, pt_d),
                       (bq, bq_d), (bk, bk_d), (bv, bv_d), (bg, bg_d),
                       (gq, gq_d), (gk, gk_d), (gCc, gC_d)]:
            dS(lambda e, t_=t_, d_=d_: e.dma_start(out=t_[:], in_=d_))
        d_base = [(f"dS{j}", cnt[f"dS{j}"]) for j in range(4)]
        dG(lambda e: e.dma_start(
            out=decay[:].rearrange("p (h n) -> p h n", h=4),
            in_=decay_d.rearrange("p (h n) -> p h n", h=4)))
        for t_, d_ in [(wq_sb, wq_d), (wk_sb, wk_d), (wv_sb, wv_d), (wg_sb, wg_d)]:
            dG(lambda e, t_=t_, d_=d_: e.dma_start(
                out=t_[:].rearrange("p (c n) -> p c n", c=16),
                in_=d_.rearrange("(c p) n -> p c n", p=128)))
        dG(lambda e: e.dma_start(
            out=wo_sb[:].rearrange("p (h n) -> p h n", h=4),
            in_=wo_d.rearrange("(h p) n -> p h n", p=128)))
        dg_wproj = [(f"dG{j}", cnt[f"dG{j}"]) for j in range(4)]

        ln_done = {}
        ln_sq = {}
        xnbuf_free = {}
        rope_used = {}
        Sbf_done = [None] * 4
        pT_free = [None]
        po_free = [None, None]
        last_sc_evac = [None]

        def emit_ln(gt, src_dram, row0, slot, xnT_dst, col0):
            W("y", "vS", ln_done.get(gt - 2))
            W("y", "sS", ln_sq.get(gt - 2))
            dv = dS(lambda e, src_dram=src_dram, row0=row0, slot=slot:
                    e.dma_start(out=xbuf[slot][:],
                                in_=src_dram[row0:row0 + 128, :]))
            Wt("v", dv)
            I("v", lambda e, slot=slot: e.tensor_reduce(
                msum[slot][:], xbuf[slot][:], AX.X, A.add), "vS")
            W("s", "vS", cnt["vS"])
            for tok in d_base:
                Wt("s", tok)
            I("s", lambda e, slot=slot: e.activation(
                mneg[slot][:], msum[slot][:], AF.Identity, scale=cst[:, 0:1]), "sS")
            W("s", "sS", cnt["sS"])
            I("s", lambda e, slot=slot: e.activation(
                scr[:], xbuf[slot][:], AF.Square, bias=mneg[slot][:],
                accum_out=sdv[slot][:]), "sS")
            ln_sq[gt] = cnt["sS"]
            W("s", "sS", cnt["sS"])
            I("s", lambda e, slot=slot: e.activation(
                msum[slot][:], sdv[slot][:], AF.Sqrt, scale=cst[:, 1:2],
                bias=cst[:, 4:5]), "sS")
            W("v", "sS", cnt["sS"])
            I("v", lambda e, slot=slot: e.reciprocal(rstd[slot][:], msum[slot][:]),
              "vS")
            W("v", "vS", cnt["vS"])
            W("v", "tS", xnbuf_free.get(gt - 2))
            lv = I("v", lambda e, slot=slot: e.tensor_scalar(
                xn_bf[slot][:], xbuf[slot][:], mneg[slot][:], rstd[slot][:],
                A.add, A.mult), "vS")
            ln_done[gt] = lv
            for grp in range(4):
                for tok in d_base:
                    Wt("t", tok)
                W("t", "vS", lv)
                W("t", "vS", pT_free[0])
                for j in range(4):
                    jj = grp * 4 + j
                    I("t", lambda e, jj=jj, j=j, slot=slot: e.transpose(
                        pT[:, j * 128:(j + 1) * 128],
                        xn_bf[slot][:, jj * 128:(jj + 1) * 128], idn[:]),
                      "tS" if j == 3 else None)
                W("v", "tS", cnt["tS"])
                pT_free[0] = I("v", lambda e, grp=grp, xnT_dst=xnT_dst, col0=col0:
                               e.tensor_copy(
                    xnT_dst[:].rearrange("p (c n) -> p c n", c=16)
                    [:, grp * 4:grp * 4 + 4, col0:col0 + 128],
                    pT[:].rearrange("p (c n) -> p c n", c=4)), "vS")
            xnbuf_free[gt] = cnt["tS"]

        partial_t_prev = [None]
        last_supd = [None]
        for ch in range(NCHD):
            W("y", "vS", rope_used.get(ch - 1))
            dr = dS(lambda e, ch=ch: e.dma_start(
                out=ropeb[0][:].rearrange("p (t n) -> p t n", t=4),
                in_=rope_d.rearrange("p (t n) -> p t n", t=4)
                [:, :, ch * C:(ch + 1) * C]))
            for ti in range(NT):
                gt = ch * NT + ti
                emit_ln(gt, x_full, gt * 128, gt % 2, xnT, ti * 128)
            xnT_done = cnt["vS"]

            rv = ropeb[0][:].rearrange("p (t n) -> p t n", t=4)
            for h in range(4):
                for tok in dg_wproj:
                    Wt("t", tok)
                W("t", "vS", xnT_done)
                W("t", "sS", last_sc_evac[0])
                for kc in range(16):
                    I("t", lambda e, kc=kc, h=h: e.matmul(
                        pP1[:], wq_sb[:, kc * 512 + h * 128: kc * 512 + (h + 1) * 128],
                        xnT[:, kc * C:(kc + 1) * C],
                        start=(kc == 0), stop=(kc == 15)), "tS" if kc == 15 else None)
                W("s", "tS", cnt["tS"])
                eq = I("s", lambda e, h=h: e.activation(
                    q0[:], pP1[:], AF.Identity, bias=bq[:, h:h + 1]), "sS")
                for kc in range(16):
                    I("t", lambda e, kc=kc, h=h: e.matmul(
                        pP2[:], wk_sb[:, kc * 512 + h * 128: kc * 512 + (h + 1) * 128],
                        xnT[:, kc * C:(kc + 1) * C],
                        start=(kc == 0), stop=(kc == 15)), "tS" if kc == 15 else None)
                W("s", "tS", cnt["tS"])
                ek = I("s", lambda e, h=h: e.activation(
                    k0[:], pP2[:], AF.Identity, bias=bk[:, h:h + 1]), "sS")
                W("t", "sS", eq)
                I("t", lambda e: e.matmul(pP1[:], ptm[:], q0[:],
                                          start=True, stop=True), "tS")
                trq = cnt["tS"]
                W("t", "sS", ek)
                I("t", lambda e: e.matmul(pP2[:], ptm[:], k0[:],
                                          start=True, stop=True), "tS")
                trk = cnt["tS"]
                Wt("v", dr)
                W("v", "tS", trq)
                I("v", lambda e: e.tensor_tensor(tmp1[:], q0[:], rv[:, 0, :],
                                                 A.mult), "vS")
                I("v", lambda e: e.tensor_tensor(tmp2[:], pP1[:], rv[:, 1, :],
                                                 A.mult), "vS")
                W("v", "vS", cnt["vS"])
                vq = I("v", lambda e: e.tensor_tensor(qr[:], tmp1[:], tmp2[:],
                                                      A.add), "vS")
                W("v", "tS", trk)
                W("v", "vS", cnt["vS"])
                I("v", lambda e: e.tensor_tensor(tmp1[:], k0[:], rv[:, 2, :],
                                                 A.mult), "vS")
                I("v", lambda e: e.tensor_tensor(tmp2[:], pP2[:], rv[:, 3, :],
                                                 A.mult), "vS")
                W("v", "vS", cnt["vS"])
                vk = I("v", lambda e: e.tensor_tensor(kr[:], tmp1[:], tmp2[:],
                                                      A.add), "vS")
                rope_used[ch] = vk

                W("t", "vS", vq)
                for kc in range(16):
                    I("t", lambda e, kc=kc, h=h: e.matmul(
                        pP1[:], wv_sb[:, kc * 512 + h * 128: kc * 512 + (h + 1) * 128],
                        xnT[:, kc * C:(kc + 1) * C],
                        start=(kc == 0), stop=(kc == 15)), "tS" if kc == 15 else None)
                W("s", "tS", cnt["tS"])
                ev = I("s", lambda e, h=h: e.activation(
                    vT_sb[:], pP1[:], AF.Identity, bias=bv[:, h:h + 1]), "sS")
                W("t", "sS", ev)
                W("t", "vS", pT_free[0])
                for kt in range(4):
                    I("t", lambda e, kt=kt: e.transpose(
                        pT[:, kt * 128:(kt + 1) * 128],
                        vT_sb[:, kt * 128:(kt + 1) * 128], idn[:]),
                      "tS" if kt == 3 else None)
                W("v", "tS", cnt["tS"])
                pT_free[0] = I("v", lambda e: e.tensor_copy(v_a[:], pT[:]), "vS")
                va = pT_free[0]

                W("t", "vS", vk)
                msk_done = [None, None]
                for kt in range(4):
                    psum = pS1 if kt % 2 == 0 else pS2
                    W("t", "vS", msk_done[kt % 2])
                    I("t", lambda e, kt=kt, psum=psum: e.matmul(
                        psum[:], kr[:, kt * 128:(kt + 1) * 128], qr[:],
                        start=True, stop=True), "tS")
                    W("v", "tS", cnt["tS"])
                    off = 384 - 128 * kt
                    mv = I("v", lambda e, kt=kt, psum=psum, off=off, h=h:
                           e.tensor_tensor(
                        scm[:, kt * C:(kt + 1) * C], psum[:],
                        decay[:].rearrange("p (h n) -> p h n", h=4)
                        [:, h, off:off + 512], A.mult), "vS")
                    msk_done[kt % 2] = mv
                scm_done = cnt["vS"]

                W("t", "vS", va)
                for kt in range(4):
                    I("t", lambda e, kt=kt: e.transpose(
                        pT[:, kt * 128:(kt + 1) * 128],
                        kr[:, kt * 128:(kt + 1) * 128], idn[:]),
                      "tS" if kt == 3 else None)
                W("s", "tS", cnt["tS"])
                for kt in range(4):
                    I("s", lambda e, kt=kt, h=h: e.activation(
                        k_a[:, kt * 128:(kt + 1) * 128],
                        pT[:, kt * 128:(kt + 1) * 128],
                        AF.Identity, scale=gk[:, h * 4 + kt:h * 4 + kt + 1]), "sS")
                ka_done = cnt["sS"]
                pT_free[0] = None
                ka_sS = ka_done

                gn_last = None
                for tt in range(4):
                    if ch > 0 and not DBG_NO_U:
                        W("t", "vS", Sbf_done[h])
                        I("t", lambda e, tt=tt, h=h: e.matmul(
                            pS1[:, 0:128], qr[:, tt * 128:(tt + 1) * 128],
                            Sbf[:, h * 128:(h + 1) * 128],
                            start=True, stop=True), "tS")
                        W("v", "tS", cnt["tS"])
                        eu = I("v", lambda e, tt=tt, h=h: e.tensor_scalar(
                            u_s[:], pS1[:, 0:128],
                            gq[:, h * 4 + tt:h * 4 + tt + 1], None,
                            A.mult, A.bypass), "vS")
                    W("t", "vS", scm_done if tt == 0 else None)
                    for kt in range(tt + 1):
                        I("t", lambda e, kt=kt, tt=tt: e.matmul(
                            pM[:, 128:256],
                            scm[:, kt * C + tt * 128: kt * C + (tt + 1) * 128],
                            v_a[:, kt * 128:(kt + 1) * 128],
                            start=(kt == 0), stop=(kt == tt)),
                          "tS" if kt == tt else None)
                    W("v", "tS", cnt["tS"])
                    if ch > 0 and not DBG_NO_U:
                        W("v", "vS", eu)
                        I("v", lambda e: e.tensor_tensor(
                            rg[:], pM[:, 128:256], u_s[:], A.add), "vS")
                    else:
                        I("v", lambda e: e.tensor_copy(rg[:], pM[:, 128:256]), "vS")
                    W("v", "vS", cnt["vS"])
                    I("v", lambda e: e.tensor_reduce(mn128[:], rg[:], AX.X, A.add),
                      "vS")
                    W("s", "vS", cnt["vS"])
                    I("s", lambda e: e.activation(
                        mn128[:], mn128[:], AF.Identity, scale=cst[:, 2:3]), "sS")
                    W("s", "sS", cnt["sS"])
                    I("s", lambda e: e.activation(
                        scr128[:], rg[:], AF.Square, bias=mn128[:],
                        accum_out=ss128[:]), "sS")
                    W("s", "sS", cnt["sS"])
                    I("s", lambda e: e.activation(
                        sd128[:], ss128[:], AF.Sqrt, scale=cst[:, 3:4],
                        bias=cst[:, 4:5]), "sS")
                    W("v", "sS", cnt["sS"])
                    I("v", lambda e: e.reciprocal(rs128[:], sd128[:]), "vS")
                    W("v", "vS", cnt["vS"])
                    gv = I("v", lambda e: e.tensor_scalar(
                        gn_bf[:], rg[:], mn128[:], rs128[:], A.add, A.mult), "vS")
                    W("t", "vS", gv)
                    W("t", "sS", ka_sS)
                    I("t", lambda e: e.transpose(pT[:, 0:128], gn_bf[:], idn[:]),
                      "tS")
                    W("v", "tS", cnt["tS"])
                    gn_last = I("v", lambda e, h=h, tt=tt: e.tensor_copy(
                        gnT[:, h * C + tt * 128: h * C + (tt + 1) * 128],
                        pT[:, 0:128]), "vS")
                pT_free[0] = gn_last

                W("t", "sS", ka_done)
                W("t", "vS", last_supd[0])
                for kt in range(4):
                    I("t", lambda e, kt=kt: e.matmul(
                        pM[:, 256:384], k_a[:, kt * 128:(kt + 1) * 128],
                        v_a[:, kt * 128:(kt + 1) * 128],
                        start=(kt == 0), stop=(kt == 3)), "tS" if kt == 3 else None)
                st_mm = cnt["tS"]
                if ch > 0 and not DBG_NO_SUPD:
                    W("s", "vS", last_supd[0])
                    es = I("s", lambda e, h=h: e.activation(
                        Stmp[:], Sst[:, h * 128:(h + 1) * 128], AF.Identity,
                        scale=gCc[:, h:h + 1]), "sS")
                    W("v", "sS", es)
                    W("v", "tS", st_mm)
                    I("v", lambda e, h=h: e.tensor_tensor(
                        Sst[:, h * 128:(h + 1) * 128], Stmp[:], pM[:, 256:384],
                        A.add), "vS")
                else:
                    W("v", "tS", st_mm)
                    I("v", lambda e, h=h: e.tensor_copy(
                        Sst[:, h * 128:(h + 1) * 128], pM[:, 256:384]), "vS")
                last_supd[0] = cnt["vS"]
                W("v", "vS", cnt["vS"])
                Sbf_done[h] = I("v", lambda e, h=h: e.tensor_copy(
                    Sbf[:, h * 128:(h + 1) * 128],
                    Sst[:, h * 128:(h + 1) * 128]), "vS")

                for kc in range(16):
                    I("t", lambda e, kc=kc, h=h: e.matmul(
                        pP2[:], wg_sb[:, kc * 512 + h * 128: kc * 512 + (h + 1) * 128],
                        xnT[:, kc * C:(kc + 1) * C],
                        start=(kc == 0), stop=(kc == 15)), "tS" if kc == 15 else None)
                W("s", "tS", cnt["tS"])
                eg = I("s", lambda e, h=h: e.activation(
                    gs[:], pP2[:], AF.Silu, bias=bg[:, h:h + 1]), "sS")
                last_sc_evac[0] = eg
                W("v", "sS", eg)
                W("v", "tS", partial_t_prev[0])
                I("v", lambda e, h=h: e.tensor_tensor(
                    preT[:, h * C:(h + 1) * C], gs[:], gnT[:, h * C:(h + 1) * C],
                    A.mult), "vS")
            preT_done = cnt["vS"]

            W("t", "vS", preT_done)
            pbuf_free = [None, None]
            for tt in range(4):
                for dc in range(4):
                    idx = tt * 4 + dc
                    psum = pO1 if idx % 2 == 0 else pO2
                    W("t", "vS", po_free[idx % 2])
                    for h in range(4):
                        I("t", lambda e, h=h, tt=tt, dc=dc, psum=psum: e.matmul(
                            psum[:], preT[:, h * C + tt * 128: h * C + (tt + 1) * 128],
                            wo_sb[:, h * D + dc * 512: h * D + (dc + 1) * 512],
                            start=(h == 0), stop=(h == 3)), "tS" if h == 3 else None)
                    W("v", "tS", cnt["tS"])
                    slot = idx % 2
                    Wt("v", pbuf_free[slot])
                    pv = I("v", lambda e, psum=psum, slot=slot: e.tensor_copy(
                        pbuf[slot][:], psum[:]), "vS")
                    po_free[idx % 2] = pv
                    W("y", "vS", pv)
                    r0 = ch * C + tt * 128
                    pbuf_free[slot] = dS(lambda e, slot=slot, r0=r0, dc=dc:
                                         e.dma_start(
                        out=cc_in[r0:r0 + 128, dc * 512:(dc + 1) * 512],
                        in_=pbuf[slot][:]))
            partial_t_prev[0] = cnt["tS"]

        for j in range(4):
            W("g", f"dS{j}", cnt[f"dS{j}"])
        I("g", lambda e: e.collective_compute(
            "ReduceScatter", A.add,
            replica_groups=[[0, 1, 2, 3], [4, 5, 6, 7]],
            ins=[cc_in_t.ap().opt()],
            outs=[cc_out_t.ap().opt()]), "cc")

        db_scales = [dS(lambda e, t_=t_, d_=d_: e.dma_start(out=t_[:], in_=d_))
                     for t_, d_ in [(sg_sb, sg_d), (su_sb, su_d),
                                    (sdbc, sdbc_d), (bobc, bobc_d)]]

        wgv = wgate_d.rearrange("(c p) n -> p c n", p=128)
        wuv = wup_d.rearrange("(c p) n -> p c n", p=128)
        wt_free = [None, None]
        wdn_free = [None] * QI
        fuse_free = [None]
        gs_free = [None, None]
        psum_free = [None, None]
        out_dmas = [0]
        ar_done = [None]
        yout_tok = [None, None]
        acc_wm = [0]
        NQ = 64 // QI

        for sp in range(NSPD):
            for ti in range(NT):
                emit_ln(200 + sp * NT + ti, x_own, sp * C + ti * 128, ti % 2,
                        xnT_sp, ti * 128)
            xnsp_done = cnt["vS"]
            for qb in range(min(NQ, NQD)):
                dg3 = None
                for icl in range(QI):
                    ic = qb * QI + icl
                    slot = ic % 2
                    pG = pP1 if ic % 2 == 0 else pS1
                    pU = pP2 if ic % 2 == 0 else pS2
                    W("g", "tS", wt_free[slot])
                    dG(lambda e, ic=ic, slot=slot: e.dma_start(
                        out=wgt[slot][:].rearrange("p (c n) -> p c n", c=16),
                        in_=wgv[:, :, ic * 128:(ic + 1) * 128]))
                    dg2 = dG(lambda e, ic=ic, slot=slot: e.dma_start(
                        out=wut[slot][:].rearrange("p (c n) -> p c n", c=16),
                        in_=wuv[:, :, ic * 128:(ic + 1) * 128]))
                    W("g", "tS", wdn_free[icl])
                    dg3 = dG(lambda e, ic=ic, icl=icl: e.dma_start(
                        out=wdnb[icl][:],
                        in_=wdn_d[ic * 128:(ic + 1) * 128, :]))
                    Wt("t", dg2)
                    W("t", "vS", xnsp_done)
                    W("t", "sS", psum_free[ic % 2])
                    for kc in range(16):
                        I("t", lambda e, kc=kc, slot=slot, pG=pG: e.matmul(
                            pG[:], wgt[slot][:, kc * 128:(kc + 1) * 128],
                            xnT_sp[:, kc * C:(kc + 1) * C],
                            start=(kc == 0), stop=(kc == 15)),
                          "tS" if kc == 15 else None)
                    tg = cnt["tS"]
                    for kc in range(16):
                        I("t", lambda e, kc=kc, slot=slot, pU=pU: e.matmul(
                            pU[:], wut[slot][:, kc * 128:(kc + 1) * 128],
                            xnT_sp[:, kc * C:(kc + 1) * C],
                            start=(kc == 0), stop=(kc == 15)),
                          "tS" if kc == 15 else None)
                    wt_free[slot] = cnt["tS"]
                    W("s", "tS", tg)
                    for tok in db_scales:
                        Wt("s", tok)
                    W("s", "vS", gs_free[slot])
                    I("s", lambda e, ic=ic, slot=slot, pG=pG: e.activation(
                        gs_b[slot][:], pG[:], AF.Silu,
                        scale=sg_sb[:, ic:ic + 1]), "sS")
                    W("s", "tS", wt_free[slot])
                    psum_free[ic % 2] = I(
                        "s", lambda e, ic=ic, slot=slot, pU=pU: e.activation(
                            gu_b[slot][:], pU[:], AF.Identity,
                            scale=su_sb[:, ic:ic + 1]), "sS")
                    W("v", "sS", psum_free[ic % 2])
                    W("v", "tS", fuse_free[0])
                    gs_free[slot] = I("v", lambda e, icl=icl, slot=slot:
                                      e.tensor_tensor(
                        fuse[:].rearrange("p (c n) -> p c n", c=QI)[:, icl, :],
                        gs_b[slot][:], gu_b[slot][:], A.mult), "vS")
                fuse_done = cnt["vS"]
                W("v", "vS", acc_wm[0])
                W("t", "vS", fuse_done)
                Wt("t", dg3)
                for tt in range(4):
                    for dc in range(4):
                        idx = tt * 4 + dc
                        psum = pO1 if idx % 2 == 0 else pO2
                        W("t", "vS", po_free[idx % 2])
                        for icl in range(QI):
                            I("t", lambda e, icl=icl, tt=tt, dc=dc, psum=psum:
                              e.matmul(
                                psum[:],
                                fuse[:].rearrange("p (c n) -> p c n", c=QI)
                                [:, icl, tt * 128:(tt + 1) * 128],
                                wdnb[icl][:, dc * 512:(dc + 1) * 512],
                                start=(icl == 0), stop=(icl == QI - 1)),
                              "tS" if icl == QI - 1 else None)
                        W("v", "tS", cnt["tS"])
                        if qb == 0:
                            po_free[idx % 2] = I(
                                "v", lambda e, tt=tt, dc=dc, psum=psum:
                                e.tensor_copy(
                                    ffn_acc[tt][:, dc * 512:(dc + 1) * 512],
                                    psum[:]), "vS")
                        else:
                            po_free[idx % 2] = I(
                                "v", lambda e, tt=tt, dc=dc, psum=psum:
                                e.tensor_tensor(
                                    ffn_acc[tt][:, dc * 512:(dc + 1) * 512],
                                    ffn_acc[tt][:, dc * 512:(dc + 1) * 512],
                                    psum[:], A.add), "vS")
                for w in range(QI):
                    wdn_free[w] = cnt["tS"]
                fuse_free[0] = cnt["tS"]
                acc_wm[0] = cnt["vS"]
            for tt in range(4):
                slot = tt % 2
                W("y", "cc", 1)
                W("y", "vS", ar_done[0])
                r0 = sp * C + tt * 128
                da = dS(lambda e, r0=r0: e.dma_start(
                    out=ar_b[0][:], in_=cc_out[r0:r0 + 128, :]))
                dx = dS(lambda e, r0=r0: e.dma_start(
                    out=xo_b[0][:], in_=x_own[r0:r0 + 128, :]))
                Wt("v", da)
                Wt("v", dx)
                for tok in db_scales:
                    Wt("v", tok)
                Wt("v", yout_tok[slot])
                W("v", "vS", acc_wm[0])
                I("v", lambda e, tt=tt, slot=slot: e.tensor_tensor(
                    y_b[slot][:], ffn_acc[tt][:], sdbc[:], A.mult), "vS")
                W("v", "vS", cnt["vS"])
                I("v", lambda e, slot=slot: e.tensor_tensor(
                    y_b[slot][:], y_b[slot][:], bobc[:], A.add), "vS")
                W("v", "vS", cnt["vS"])
                I("v", lambda e, slot=slot: e.tensor_tensor(
                    y_b[slot][:], y_b[slot][:], xo_b[0][:], A.add), "vS")
                W("v", "vS", cnt["vS"])
                yv = I("v", lambda e, slot=slot: e.tensor_tensor(
                    y_b[slot][:], y_b[slot][:], ar_b[0][:], A.add), "vS")
                ar_done[0] = yv
                W("y", "vS", yv)
                yout_tok[slot] = dO(lambda e, slot=slot, r0=r0: e.dma_start(
                    out=y_d[r0:r0 + 128, :], in_=y_b[slot][:]))
                out_dmas[0] += 1

        for j in range(2):
            ops["y"].append(lambda e, j=j: e.wait_ge(sems[f"dO{j}"], cnt[f"dO{j}"]))

        with nc.Block() as block:
            @block.tensor
            def _(t):
                for f in ops["t"]:
                    f(t)

            @block.vector
            def _(v):
                for f in ops["v"]:
                    f(v)

            @block.scalar
            def _(s):
                for f in ops["s"]:
                    f(s)

            @block.gpsimd
            def _(g):
                for f in ops["g"]:
                    f(g)

            @block.sync
            def _(y):
                for f in ops["y"]:
                    f(y)

    return nc


def host_tables():
    pos = np.arange(S, dtype=np.float64)
    invf = 1.0 / THETA ** (np.arange(0, DK, 2, dtype=np.float64) / DK)
    j = np.arange(DK) // 2
    ang = pos[None, :] * invf[j][:, None]
    cos, sin = np.cos(ang), np.sin(ang)
    sbase = (np.arange(0, DK, 2, dtype=np.float64) + 0.4 * DK) / (1.4 * DK)
    scale = sbase[j][:, None] ** ((pos - S // 2) / SBASE)[None, :]
    ropes = np.concatenate([cos * scale, sin * scale, cos / scale, sin / scale],
                           axis=1).astype(np.float32)
    P = np.zeros((DK, DK), np.float64)
    for jj in range(0, DK, 2):
        P[jj, jj + 1] = -1.0
        P[jj + 1, jj] = 1.0
    PT = np.ascontiguousarray(P.T).astype(ml_dtypes.bfloat16)
    ident = np.eye(128, dtype=np.float32).astype(ml_dtypes.bfloat16)
    consts = np.ascontiguousarray(np.broadcast_to(np.array(
        [-1.0 / D, 1.0 / D, -1.0 / DK, 1.0 / DK, EPS], np.float32), (128, 5)))
    return ropes, PT, ident, consts


def decay_tables(heads):
    T4, gq4, gk4, gC4 = [], [], [], []
    p = np.arange(128, dtype=np.float64)
    for h in heads:
        g = 1.0 - 2.0 ** (-5.0 - float(h))
        d = np.arange(896, dtype=np.float64)[None, :] - 384.0 - p[:, None]
        T4.append(np.where(d >= 0, g ** np.maximum(d, 0.0), 0.0))
        gq4.append(np.stack([g ** (128.0 * t + p + 1.0) for t in range(4)], 1))
        gk4.append(np.stack([g ** (511.0 - 128.0 * kt - p) for kt in range(4)], 1))
        gC4.append(np.full(128, g ** C))
    return (np.concatenate(T4, 1).astype(np.float32),
            np.concatenate(gq4, 1).astype(np.float32),
            np.concatenate(gk4, 1).astype(np.float32),
            np.stack(gC4, 1).astype(np.float32))


_NC = None


def kernel(L0_attn_x, L0_attn_Wq, L0_attn_bq, L0_attn_Wk, L0_attn_bk,
           L0_attn_Wv, L0_attn_bv, L0_attn_Wg, L0_attn_bg, L0_attn_Wo,
           L0_attn_bo, L0_ffn_w_gate, L0_ffn_w_up, L0_ffn_w_down,
           L0_ffn_s_gate, L0_ffn_s_up, L0_ffn_s_down):
    global _NC
    if _NC is None:
        _NC = build_nc()
    nc = _NC
    ropes, PT, ident, consts = host_tables()
    x = np.ascontiguousarray(np.asarray(L0_attn_x, np.float32))
    f32 = lambda a: np.ascontiguousarray(np.asarray(a, np.float32))
    i8 = lambda a: np.ascontiguousarray(np.asarray(a, np.int8))
    col4 = lambda a, hg: np.ascontiguousarray(
        f32(a)[4 * hg * 128:(4 * hg + 4) * 128].reshape(4, 128).T)
    sg_c = np.ascontiguousarray(f32(L0_ffn_s_gate).reshape(64, 128).T)
    su_c = np.ascontiguousarray(f32(L0_ffn_s_up).reshape(64, 128).T)
    sd_bc = np.ascontiguousarray(np.broadcast_to(f32(L0_ffn_s_down), (128, D)))
    bo_bc = np.ascontiguousarray(np.broadcast_to(f32(L0_attn_bo), (128, D)))
    wg_full = i8(L0_ffn_w_gate)
    wu_full = i8(L0_ffn_w_up)
    wd_full = i8(L0_ffn_w_down)
    in_maps = []
    for c in range(8):
        b, hg, qt = c // 4, c % 4, c % 4
        heads = list(range(4 * hg, 4 * hg + 4))
        decayT, gq_c, gk_c, gC_c = decay_tables(heads)
        sl = slice(4 * hg * 128, (4 * hg + 4) * 128)
        in_maps.append(dict(
            x_full=x[b],
            x_own=np.ascontiguousarray(x[b, qt * KB:(qt + 1) * KB]),
            wq_s=np.ascontiguousarray(f32(L0_attn_Wq)[:, sl]),
            wk_s=np.ascontiguousarray(f32(L0_attn_Wk)[:, sl]),
            wv_s=np.ascontiguousarray(f32(L0_attn_Wv)[:, sl]),
            wg_s=np.ascontiguousarray(f32(L0_attn_Wg)[:, sl]),
            wo_s=np.ascontiguousarray(f32(L0_attn_Wo)[sl, :]),
            bq_c=col4(L0_attn_bq, hg), bk_c=col4(L0_attn_bk, hg),
            bv_c=col4(L0_attn_bv, hg), bg_c=col4(L0_attn_bg, hg),
            w_gate=wg_full, w_up=wu_full, w_down=wd_full,
            sg_c=sg_c, su_c=su_c, sd_bc=sd_bc, bo_bc=bo_bc,
            ropes=ropes, decayT=decayT, gq_c=gq_c, gk_c=gk_c, gC_c=gC_c,
            consts=consts, ident=ident, PT=PT,
        ))
    res = run_bass_kernel_spmd(nc, in_maps, core_ids=list(range(8)))
    out = np.empty((B, S, D), np.float32)
    for c in range(8):
        out[c // 4, (c % 4) * KB:(c % 4 + 1) * KB] = res.results[c]["y"]
    return out


# version 03
# speedup vs baseline: 1.0886x; 1.0310x over previous
import numpy as np
import ml_dtypes
import concourse.bass as bass
import concourse.mybir as mybir
from concourse.bass_utils import run_bass_kernel_spmd
from contextlib import ExitStack

F32 = mybir.dt.float32
BF16 = mybir.dt.bfloat16
I8 = mybir.dt.int8
A = mybir.AluOpType
AF = mybir.ActivationFunctionType
AX = mybir.AxisListType

B, S, D, H, DK, INTER = 2, 4096, 2048, 16, 128, 8192
C = 512
NCH = S // C
NT = C // 128
EPS = 1e-5
THETA = 1e4
SBASE = 512.0
KB = 1024
NCHD = NCH
NQD = 16
NSPD = 2
DBG_NO_U = False
DBG_U_LEVEL = 3
DBG_NO_SUPD = False


def build_nc():
    nc = bass.Bass("TRN2", target_bir_lowering=False, debug=False)

    def din(name, shape, dt):
        return nc.dram_tensor(name, shape, dt, kind="ExternalInput").ap()

    x_full = din("x_full", [S, D], F32)
    x_own = din("x_own", [KB, D], F32)
    wq_d = din("wq_s", [D, 512], F32)
    wk_d = din("wk_s", [D, 512], F32)
    wv_d = din("wv_s", [D, 512], F32)
    wg_d = din("wg_s", [D, 512], F32)
    wo_d = din("wo_s", [512, D], F32)
    bq_d = din("bq_c", [128, 4], F32)
    bk_d = din("bk_c", [128, 4], F32)
    bv_d = din("bv_c", [128, 4], F32)
    bg_d = din("bg_c", [128, 4], F32)
    wgate_d = din("w_gate", [D, INTER], I8)
    wup_d = din("w_up", [D, INTER], I8)
    wdn_d = din("w_down", [INTER, D], I8)
    sg_d = din("sg_c", [128, 64], F32)
    su_d = din("su_c", [128, 64], F32)
    sdbc_d = din("sd_bc", [128, D], F32)
    bobc_d = din("bo_bc", [128, D], F32)
    rope_d = din("ropes", [128, 4 * S], F32)
    decay_d = din("decayT", [128, 4 * 896], F32)
    gq_d = din("gq_c", [128, 16], F32)
    gk_d = din("gk_c", [128, 16], F32)
    gC_d = din("gC_c", [128, 4], F32)
    cst_d = din("consts", [128, 5], F32)
    id_d = din("ident", [128, 128], BF16)
    pt_d = din("PT", [128, 128], BF16)

    y_d = nc.dram_tensor("y", [KB, D], F32, kind="ExternalOutput").ap()
    cc_in_t = [nc.dram_tensor(f"cc_in{j}", [S // 2, D], BF16) for j in range(2)]
    cc_out_t = [nc.dram_tensor(f"cc_out{j}", [KB // 2, D], BF16) for j in range(2)]
    cc_in = [t.ap() for t in cc_in_t]
    cc_out = [t.ap() for t in cc_out_t]

    with ExitStack() as st:
        E = st.enter_context
        ARENA = 190 * KB
        arena = E(nc.sbuf_tensor("arena", [128, ARENA], I8))
        cur = [0]

        def sb(cols, dt):
            nb = cols * {F32: 4, BF16: 2, I8: 1}[dt]
            nb = (nb + 511) // 512 * 512
            off = cur[0]
            cur[0] += nb
            assert cur[0] <= ARENA, f"arena overflow {cur[0]}"
            return arena[:, off:off + cols * {F32: 4, BF16: 2, I8: 1}[dt]].bitcast(dt)

        xbuf = [sb(D, F32) for _ in range(2)]
        scr = sb(D, F32)
        msum = [sb(1, F32) for _ in range(2)]
        mneg = [sb(1, F32) for _ in range(2)]
        sdv = [sb(1, F32) for _ in range(2)]
        rstd = [sb(1, F32) for _ in range(2)]
        xn_bf = [sb(D, BF16) for _ in range(2)]
        cst = sb(5, F32)
        idn = sb(128, BF16)
        ptm = sb(128, BF16)
        bq = sb(4, F32); bk = sb(4, F32); bv = sb(4, F32); bg = sb(4, F32)
        gq = sb(16, F32); gk = sb(16, F32); gCc = sb(4, F32)
        mn128 = sb(1, F32); ss128 = sb(1, F32); sd128 = sb(1, F32); rs128 = sb(1, F32)
        scr128 = sb(128, F32)
        Sst = sb(4 * 128, F32)
        Stmp = sb(128, F32)
        Sbf = sb(4 * 128, BF16)
        shared_end = cur[0]

        xnT = sb(16 * C, BF16)
        wq_sb = sb(16 * 512, BF16)
        wk_sb = sb(16 * 512, BF16)
        wv_sb = sb(16 * 512, BF16)
        wg_sb = sb(16 * 512, BF16)
        wo_sb = sb(4 * D, BF16)
        ropeb = [sb(4 * C, F32)]
        decay = sb(4 * 896, BF16)
        q0 = sb(C, BF16); k0 = sb(C, BF16)
        qr = sb(C, BF16); kr = sb(C, BF16)
        tmp1 = sb(C, F32); tmp2 = sb(C, F32)
        scm = sb(4 * C, BF16)
        vT_sb = sb(C, BF16)
        v_a = sb(4 * 128, BF16)
        k_a = sb(4 * 128, BF16)
        u_s = sb(128, F32)
        rg = sb(128, F32)
        gn_bf = sb(128, BF16)
        gnT = sb(4 * C, BF16)
        gs = sb(C, F32)
        preT = sb(4 * C, BF16)
        pbuf = [sb(512, BF16) for _ in range(2)]
        assert cur[0] <= ARENA

        cur[0] = shared_end
        QI = 4
        xnT_sp = sb(16 * C, BF16)
        fuse = sb(QI * C, BF16)
        ffn_acc = [sb(D, F32) for _ in range(4)]
        wdnb = [sb(D, BF16) for _ in range(QI)]
        wgt = [sb(16 * 128, BF16) for _ in range(2)]
        wut = [sb(16 * 128, BF16) for _ in range(2)]
        sg_sb = sb(64, F32); su_sb = sb(64, F32)
        sdbc = sb(D, F32); bobc = sb(D, F32)
        gs_b = [sb(C, F32) for _ in range(2)]
        gu_b = [sb(C, F32) for _ in range(2)]
        xo_b = [sb(D, F32)]
        ar_b = [sb(D, BF16)]
        y_b = [sb(D, F32) for _ in range(2)]
        assert cur[0] <= ARENA, f"phase B overflow {cur[0]}"

        pP1 = E(nc.psum_tensor("pP1", [128, 512], F32))
        pP2 = E(nc.psum_tensor("pP2", [128, 512], F32))
        pS1 = E(nc.psum_tensor("pS1", [128, 512], F32))
        pS2 = E(nc.psum_tensor("pS2", [128, 512], F32))
        pM = E(nc.psum_tensor("pM", [128, 512], F32))
        pT = E(nc.psum_tensor("pT", [128, 512], BF16))
        pO1 = E(nc.psum_tensor("pO1", [128, 512], F32))
        pO2 = E(nc.psum_tensor("pO2", [128, 512], F32))

        semkeys = (["tS", "vS", "sS", "cc"]
                   + [f"dS{j}" for j in range(4)]
                   + [f"dG{j}" for j in range(4)]
                   + [f"dO{j}" for j in range(2)])
        sems = {k: E(nc.semaphore(k)) for k in semkeys}

        ops = {k: [] for k in "tvsgy"}
        cnt = {k: 0 for k in sems}
        waited = {}

        def W(eng, sk, val):
            if val is None or val <= 0:
                return
            if waited.get((eng, sk), 0) >= val:
                return
            waited[(eng, sk)] = val
            ops[eng].append(lambda e, sk=sk, v=val: e.wait_ge(sems[sk], v))

        def Wt(eng, tok):
            if tok is not None:
                W(eng, tok[0], tok[1])

        def I(eng, fn, sk=None, n=1):
            if sk:
                cnt[sk] += n

            def f(e, fn=fn, sk=sk, n=n):
                r = fn(e)
                if sk:
                    r.then_inc(sems[sk], n)
            ops[eng].append(f)
            return cnt[sk] if sk else None

        def mk_stream(qeng, prefix, depth):
            idx = [0]

            def issue(fn):
                j = idx[0] % depth
                sk = f"{prefix}{j}"
                if cnt[sk] > 0:
                    W(qeng, sk, cnt[sk])
                I(qeng, fn, sk, 16)
                idx[0] += 1
                return (sk, cnt[sk])
            return issue

        dS = mk_stream("y", "dS", 4)
        dG = mk_stream("g", "dG", 4)
        dO = mk_stream("y", "dO", 2)

        for t_, d_ in [(cst, cst_d), (idn, id_d), (ptm, pt_d),
                       (bq, bq_d), (bk, bk_d), (bv, bv_d), (bg, bg_d),
                       (gq, gq_d), (gk, gk_d), (gCc, gC_d)]:
            dS(lambda e, t_=t_, d_=d_: e.dma_start(out=t_[:], in_=d_))
        d_base = [(f"dS{j}", cnt[f"dS{j}"]) for j in range(4)]
        dG(lambda e: e.dma_start(
            out=decay[:].rearrange("p (h n) -> p h n", h=4),
            in_=decay_d.rearrange("p (h n) -> p h n", h=4)))
        for t_, d_ in [(wq_sb, wq_d), (wk_sb, wk_d), (wv_sb, wv_d), (wg_sb, wg_d)]:
            dG(lambda e, t_=t_, d_=d_: e.dma_start(
                out=t_[:].rearrange("p (c n) -> p c n", c=16),
                in_=d_.rearrange("(c p) n -> p c n", p=128)))
        dG(lambda e: e.dma_start(
            out=wo_sb[:].rearrange("p (h n) -> p h n", h=4),
            in_=wo_d.rearrange("(h p) n -> p h n", p=128)))
        dg_wproj = [(f"dG{j}", cnt[f"dG{j}"]) for j in range(4)]

        ln_done = {}
        ln_sq = {}
        xnbuf_free = {}
        rope_used = {}
        Sbf_done = [None] * 4
        pT_free = [None]
        po_free = [None, None]
        last_sc_evac = [None]

        def emit_ln(gt, src_dram, row0, slot, xnT_dst, col0):
            W("y", "vS", ln_done.get(gt - 2))
            W("y", "sS", ln_sq.get(gt - 2))
            dv = dS(lambda e, src_dram=src_dram, row0=row0, slot=slot:
                    e.dma_start(out=xbuf[slot][:],
                                in_=src_dram[row0:row0 + 128, :]))
            Wt("v", dv)
            I("v", lambda e, slot=slot: e.tensor_reduce(
                msum[slot][:], xbuf[slot][:], AX.X, A.add), "vS")
            W("s", "vS", cnt["vS"])
            for tok in d_base:
                Wt("s", tok)
            I("s", lambda e, slot=slot: e.activation(
                mneg[slot][:], msum[slot][:], AF.Identity, scale=cst[:, 0:1]), "sS")
            W("s", "sS", cnt["sS"])
            I("s", lambda e, slot=slot: e.activation(
                scr[:], xbuf[slot][:], AF.Square, bias=mneg[slot][:],
                accum_out=sdv[slot][:]), "sS")
            ln_sq[gt] = cnt["sS"]
            W("s", "sS", cnt["sS"])
            I("s", lambda e, slot=slot: e.activation(
                msum[slot][:], sdv[slot][:], AF.Sqrt, scale=cst[:, 1:2],
                bias=cst[:, 4:5]), "sS")
            W("v", "sS", cnt["sS"])
            I("v", lambda e, slot=slot: e.reciprocal(rstd[slot][:], msum[slot][:]),
              "vS")
            W("v", "vS", cnt["vS"])
            W("v", "tS", xnbuf_free.get(gt - 2))
            lv = I("v", lambda e, slot=slot: e.tensor_scalar(
                xn_bf[slot][:], xbuf[slot][:], mneg[slot][:], rstd[slot][:],
                A.add, A.mult), "vS")
            ln_done[gt] = lv
            for grp in range(4):
                for tok in d_base:
                    Wt("t", tok)
                W("t", "vS", lv)
                W("t", "vS", pT_free[0])
                for j in range(4):
                    jj = grp * 4 + j
                    I("t", lambda e, jj=jj, j=j, slot=slot: e.transpose(
                        pT[:, j * 128:(j + 1) * 128],
                        xn_bf[slot][:, jj * 128:(jj + 1) * 128], idn[:]),
                      "tS" if j == 3 else None)
                W("v", "tS", cnt["tS"])
                pT_free[0] = I("v", lambda e, grp=grp, xnT_dst=xnT_dst, col0=col0:
                               e.tensor_copy(
                    xnT_dst[:].rearrange("p (c n) -> p c n", c=16)
                    [:, grp * 4:grp * 4 + 4, col0:col0 + 128],
                    pT[:].rearrange("p (c n) -> p c n", c=4)), "vS")
            xnbuf_free[gt] = cnt["tS"]

        partial_t_prev = [None]
        last_supd = [None]
        for ch in range(NCHD):
            W("y", "vS", rope_used.get(ch - 1))
            dr = dS(lambda e, ch=ch: e.dma_start(
                out=ropeb[0][:].rearrange("p (t n) -> p t n", t=4),
                in_=rope_d.rearrange("p (t n) -> p t n", t=4)
                [:, :, ch * C:(ch + 1) * C]))
            for ti in range(NT):
                gt = ch * NT + ti
                emit_ln(gt, x_full, gt * 128, gt % 2, xnT, ti * 128)
            xnT_done = cnt["vS"]

            rv = ropeb[0][:].rearrange("p (t n) -> p t n", t=4)
            for h in range(4):
                for tok in dg_wproj:
                    Wt("t", tok)
                W("t", "vS", xnT_done)
                W("t", "sS", last_sc_evac[0])
                for kc in range(16):
                    I("t", lambda e, kc=kc, h=h: e.matmul(
                        pP1[:], wq_sb[:, kc * 512 + h * 128: kc * 512 + (h + 1) * 128],
                        xnT[:, kc * C:(kc + 1) * C],
                        start=(kc == 0), stop=(kc == 15)), "tS" if kc == 15 else None)
                W("s", "tS", cnt["tS"])
                eq = I("s", lambda e, h=h: e.activation(
                    q0[:], pP1[:], AF.Identity, bias=bq[:, h:h + 1]), "sS")
                for kc in range(16):
                    I("t", lambda e, kc=kc, h=h: e.matmul(
                        pP2[:], wk_sb[:, kc * 512 + h * 128: kc * 512 + (h + 1) * 128],
                        xnT[:, kc * C:(kc + 1) * C],
                        start=(kc == 0), stop=(kc == 15)), "tS" if kc == 15 else None)
                W("s", "tS", cnt["tS"])
                ek = I("s", lambda e, h=h: e.activation(
                    k0[:], pP2[:], AF.Identity, bias=bk[:, h:h + 1]), "sS")
                W("t", "sS", eq)
                I("t", lambda e: e.matmul(pP1[:], ptm[:], q0[:],
                                          start=True, stop=True), "tS")
                trq = cnt["tS"]
                W("t", "sS", ek)
                I("t", lambda e: e.matmul(pP2[:], ptm[:], k0[:],
                                          start=True, stop=True), "tS")
                trk = cnt["tS"]
                Wt("v", dr)
                W("v", "tS", trq)
                I("v", lambda e: e.tensor_tensor(tmp1[:], q0[:], rv[:, 0, :],
                                                 A.mult), "vS")
                I("v", lambda e: e.tensor_tensor(tmp2[:], pP1[:], rv[:, 1, :],
                                                 A.mult), "vS")
                W("v", "vS", cnt["vS"])
                vq = I("v", lambda e: e.tensor_tensor(qr[:], tmp1[:], tmp2[:],
                                                      A.add), "vS")
                W("v", "tS", trk)
                W("v", "vS", cnt["vS"])
                I("v", lambda e: e.tensor_tensor(tmp1[:], k0[:], rv[:, 2, :],
                                                 A.mult), "vS")
                I("v", lambda e: e.tensor_tensor(tmp2[:], pP2[:], rv[:, 3, :],
                                                 A.mult), "vS")
                W("v", "vS", cnt["vS"])
                vk = I("v", lambda e: e.tensor_tensor(kr[:], tmp1[:], tmp2[:],
                                                      A.add), "vS")
                rope_used[ch] = vk

                W("t", "vS", vq)
                for kc in range(16):
                    I("t", lambda e, kc=kc, h=h: e.matmul(
                        pP1[:], wv_sb[:, kc * 512 + h * 128: kc * 512 + (h + 1) * 128],
                        xnT[:, kc * C:(kc + 1) * C],
                        start=(kc == 0), stop=(kc == 15)), "tS" if kc == 15 else None)
                W("s", "tS", cnt["tS"])
                ev = I("s", lambda e, h=h: e.activation(
                    vT_sb[:], pP1[:], AF.Identity, bias=bv[:, h:h + 1]), "sS")
                W("t", "sS", ev)
                W("t", "vS", pT_free[0])
                for kt in range(4):
                    I("t", lambda e, kt=kt: e.transpose(
                        pT[:, kt * 128:(kt + 1) * 128],
                        vT_sb[:, kt * 128:(kt + 1) * 128], idn[:]),
                      "tS" if kt == 3 else None)
                W("v", "tS", cnt["tS"])
                pT_free[0] = I("v", lambda e: e.tensor_copy(v_a[:], pT[:]), "vS")
                va = pT_free[0]

                W("t", "vS", vk)
                msk_done = [None, None]
                for kt in range(4):
                    psum = pS1 if kt % 2 == 0 else pS2
                    W("t", "vS", msk_done[kt % 2])
                    I("t", lambda e, kt=kt, psum=psum: e.matmul(
                        psum[:], kr[:, kt * 128:(kt + 1) * 128], qr[:],
                        start=True, stop=True), "tS")
                    W("v", "tS", cnt["tS"])
                    off = 384 - 128 * kt
                    mv = I("v", lambda e, kt=kt, psum=psum, off=off, h=h:
                           e.tensor_tensor(
                        scm[:, kt * C:(kt + 1) * C], psum[:],
                        decay[:].rearrange("p (h n) -> p h n", h=4)
                        [:, h, off:off + 512], A.mult), "vS")
                    msk_done[kt % 2] = mv
                scm_done = cnt["vS"]

                W("t", "vS", va)
                for kt in range(4):
                    I("t", lambda e, kt=kt: e.transpose(
                        pT[:, kt * 128:(kt + 1) * 128],
                        kr[:, kt * 128:(kt + 1) * 128], idn[:]),
                      "tS" if kt == 3 else None)
                W("s", "tS", cnt["tS"])
                for kt in range(4):
                    I("s", lambda e, kt=kt, h=h: e.activation(
                        k_a[:, kt * 128:(kt + 1) * 128],
                        pT[:, kt * 128:(kt + 1) * 128],
                        AF.Identity, scale=gk[:, h * 4 + kt:h * 4 + kt + 1]), "sS")
                ka_done = cnt["sS"]
                pT_free[0] = None
                ka_sS = ka_done

                gn_last = None
                for tt in range(4):
                    if ch > 0 and not DBG_NO_U:
                        W("t", "vS", Sbf_done[h])
                        I("t", lambda e, tt=tt, h=h: e.matmul(
                            pS1[:, 0:128], qr[:, tt * 128:(tt + 1) * 128],
                            Sbf[:, h * 128:(h + 1) * 128],
                            start=True, stop=True), "tS")
                        W("v", "tS", cnt["tS"])
                        eu = I("v", lambda e, tt=tt, h=h: e.tensor_scalar(
                            u_s[:], pS1[:, 0:128],
                            gq[:, h * 4 + tt:h * 4 + tt + 1], None,
                            A.mult, A.bypass), "vS")
                    W("t", "vS", scm_done if tt == 0 else None)
                    for kt in range(tt + 1):
                        I("t", lambda e, kt=kt, tt=tt: e.matmul(
                            pM[:, 128:256],
                            scm[:, kt * C + tt * 128: kt * C + (tt + 1) * 128],
                            v_a[:, kt * 128:(kt + 1) * 128],
                            start=(kt == 0), stop=(kt == tt)),
                          "tS" if kt == tt else None)
                    W("v", "tS", cnt["tS"])
                    if ch > 0 and not DBG_NO_U:
                        W("v", "vS", eu)
                        I("v", lambda e: e.tensor_tensor(
                            rg[:], pM[:, 128:256], u_s[:], A.add), "vS")
                    else:
                        I("v", lambda e: e.tensor_copy(rg[:], pM[:, 128:256]), "vS")
                    W("v", "vS", cnt["vS"])
                    I("v", lambda e: e.tensor_reduce(mn128[:], rg[:], AX.X, A.add),
                      "vS")
                    W("s", "vS", cnt["vS"])
                    I("s", lambda e: e.activation(
                        mn128[:], mn128[:], AF.Identity, scale=cst[:, 2:3]), "sS")
                    W("s", "sS", cnt["sS"])
                    I("s", lambda e: e.activation(
                        scr128[:], rg[:], AF.Square, bias=mn128[:],
                        accum_out=ss128[:]), "sS")
                    W("s", "sS", cnt["sS"])
                    I("s", lambda e: e.activation(
                        sd128[:], ss128[:], AF.Sqrt, scale=cst[:, 3:4],
                        bias=cst[:, 4:5]), "sS")
                    W("v", "sS", cnt["sS"])
                    I("v", lambda e: e.reciprocal(rs128[:], sd128[:]), "vS")
                    W("v", "vS", cnt["vS"])
                    gv = I("v", lambda e: e.tensor_scalar(
                        gn_bf[:], rg[:], mn128[:], rs128[:], A.add, A.mult), "vS")
                    W("t", "vS", gv)
                    W("t", "sS", ka_sS)
                    I("t", lambda e: e.transpose(pT[:, 0:128], gn_bf[:], idn[:]),
                      "tS")
                    W("v", "tS", cnt["tS"])
                    gn_last = I("v", lambda e, h=h, tt=tt: e.tensor_copy(
                        gnT[:, h * C + tt * 128: h * C + (tt + 1) * 128],
                        pT[:, 0:128]), "vS")
                pT_free[0] = gn_last

                W("t", "sS", ka_done)
                W("t", "vS", last_supd[0])
                for kt in range(4):
                    I("t", lambda e, kt=kt: e.matmul(
                        pM[:, 256:384], k_a[:, kt * 128:(kt + 1) * 128],
                        v_a[:, kt * 128:(kt + 1) * 128],
                        start=(kt == 0), stop=(kt == 3)), "tS" if kt == 3 else None)
                st_mm = cnt["tS"]
                if ch > 0 and not DBG_NO_SUPD:
                    W("s", "vS", last_supd[0])
                    es = I("s", lambda e, h=h: e.activation(
                        Stmp[:], Sst[:, h * 128:(h + 1) * 128], AF.Identity,
                        scale=gCc[:, h:h + 1]), "sS")
                    W("v", "sS", es)
                    W("v", "tS", st_mm)
                    I("v", lambda e, h=h: e.tensor_tensor(
                        Sst[:, h * 128:(h + 1) * 128], Stmp[:], pM[:, 256:384],
                        A.add), "vS")
                else:
                    W("v", "tS", st_mm)
                    I("v", lambda e, h=h: e.tensor_copy(
                        Sst[:, h * 128:(h + 1) * 128], pM[:, 256:384]), "vS")
                last_supd[0] = cnt["vS"]
                W("v", "vS", cnt["vS"])
                Sbf_done[h] = I("v", lambda e, h=h: e.tensor_copy(
                    Sbf[:, h * 128:(h + 1) * 128],
                    Sst[:, h * 128:(h + 1) * 128]), "vS")

                for kc in range(16):
                    I("t", lambda e, kc=kc, h=h: e.matmul(
                        pP2[:], wg_sb[:, kc * 512 + h * 128: kc * 512 + (h + 1) * 128],
                        xnT[:, kc * C:(kc + 1) * C],
                        start=(kc == 0), stop=(kc == 15)), "tS" if kc == 15 else None)
                W("s", "tS", cnt["tS"])
                eg = I("s", lambda e, h=h: e.activation(
                    gs[:], pP2[:], AF.Silu, bias=bg[:, h:h + 1]), "sS")
                last_sc_evac[0] = eg
                W("v", "sS", eg)
                W("v", "tS", partial_t_prev[0])
                I("v", lambda e, h=h: e.tensor_tensor(
                    preT[:, h * C:(h + 1) * C], gs[:], gnT[:, h * C:(h + 1) * C],
                    A.mult), "vS")
            preT_done = cnt["vS"]

            W("t", "vS", preT_done)
            pbuf_free = [None, None]
            for tt in range(4):
                for dc in range(4):
                    idx = tt * 4 + dc
                    psum = pO1 if idx % 2 == 0 else pO2
                    W("t", "vS", po_free[idx % 2])
                    for h in range(4):
                        I("t", lambda e, h=h, tt=tt, dc=dc, psum=psum: e.matmul(
                            psum[:], preT[:, h * C + tt * 128: h * C + (tt + 1) * 128],
                            wo_sb[:, h * D + dc * 512: h * D + (dc + 1) * 512],
                            start=(h == 0), stop=(h == 3)), "tS" if h == 3 else None)
                    W("v", "tS", cnt["tS"])
                    slot = idx % 2
                    Wt("v", pbuf_free[slot])
                    pv = I("v", lambda e, psum=psum, slot=slot: e.tensor_copy(
                        pbuf[slot][:], psum[:]), "vS")
                    po_free[idx % 2] = pv
                    W("y", "vS", pv)
                    r0 = (ch // 2) * C + tt * 128
                    pbuf_free[slot] = dS(lambda e, slot=slot, r0=r0, dc=dc,
                                         par=ch % 2: e.dma_start(
                        out=cc_in[par][r0:r0 + 128, dc * 512:(dc + 1) * 512],
                        in_=pbuf[slot][:]))
            if ch == NCH - 2 and NCHD == NCH:
                for j in range(4):
                    W("g", f"dS{j}", cnt[f"dS{j}"])
                I("g", lambda e: e.collective_compute(
                    "ReduceScatter", A.add,
                    replica_groups=[[0, 1, 2, 3], [4, 5, 6, 7]],
                    ins=[cc_in_t[0].ap().opt()],
                    outs=[cc_out_t[0].ap().opt()]), "cc")
            partial_t_prev[0] = cnt["tS"]

        for j in range(4):
            W("g", f"dS{j}", cnt[f"dS{j}"])
        if NCHD != NCH:
            I("g", lambda e: e.collective_compute(
                "ReduceScatter", A.add,
                replica_groups=[[0, 1, 2, 3], [4, 5, 6, 7]],
                ins=[cc_in_t[0].ap().opt()],
                outs=[cc_out_t[0].ap().opt()]), "cc")
        I("g", lambda e: e.collective_compute(
            "ReduceScatter", A.add,
            replica_groups=[[0, 1, 2, 3], [4, 5, 6, 7]],
            ins=[cc_in_t[1].ap().opt()],
            outs=[cc_out_t[1].ap().opt()]), "cc")

        db_scales = [dS(lambda e, t_=t_, d_=d_: e.dma_start(out=t_[:], in_=d_))
                     for t_, d_ in [(sg_sb, sg_d), (su_sb, su_d),
                                    (sdbc, sdbc_d), (bobc, bobc_d)]]

        wgv = wgate_d.rearrange("(c p) n -> p c n", p=128)
        wuv = wup_d.rearrange("(c p) n -> p c n", p=128)
        wt_free = [None, None]
        wdn_free = [None] * QI
        fuse_free = [None]
        gs_free = [None, None]
        psum_free = [None, None]
        out_dmas = [0]
        ar_done = [None]
        yout_tok = [None, None]
        acc_wm = [0]
        NQ = 64 // QI

        for sp in range(NSPD):
            for ti in range(NT):
                emit_ln(200 + sp * NT + ti, x_own, sp * C + ti * 128, ti % 2,
                        xnT_sp, ti * 128)
            xnsp_done = cnt["vS"]
            for qb in range(min(NQ, NQD)):
                dg3 = None
                for icl in range(QI):
                    ic = qb * QI + icl
                    slot = ic % 2
                    pG = pP1 if ic % 2 == 0 else pS1
                    pU = pP2 if ic % 2 == 0 else pS2
                    W("g", "tS", wt_free[slot])
                    dG(lambda e, ic=ic, slot=slot: e.dma_start(
                        out=wgt[slot][:].rearrange("p (c n) -> p c n", c=16),
                        in_=wgv[:, :, ic * 128:(ic + 1) * 128]))
                    dg2 = dG(lambda e, ic=ic, slot=slot: e.dma_start(
                        out=wut[slot][:].rearrange("p (c n) -> p c n", c=16),
                        in_=wuv[:, :, ic * 128:(ic + 1) * 128]))
                    W("g", "tS", wdn_free[icl])
                    dg3 = dG(lambda e, ic=ic, icl=icl: e.dma_start(
                        out=wdnb[icl][:],
                        in_=wdn_d[ic * 128:(ic + 1) * 128, :]))
                    Wt("t", dg2)
                    W("t", "vS", xnsp_done)
                    W("t", "sS", psum_free[ic % 2])
                    for kc in range(16):
                        I("t", lambda e, kc=kc, slot=slot, pG=pG: e.matmul(
                            pG[:], wgt[slot][:, kc * 128:(kc + 1) * 128],
                            xnT_sp[:, kc * C:(kc + 1) * C],
                            start=(kc == 0), stop=(kc == 15)),
                          "tS" if kc == 15 else None)
                    tg = cnt["tS"]
                    for kc in range(16):
                        I("t", lambda e, kc=kc, slot=slot, pU=pU: e.matmul(
                            pU[:], wut[slot][:, kc * 128:(kc + 1) * 128],
                            xnT_sp[:, kc * C:(kc + 1) * C],
                            start=(kc == 0), stop=(kc == 15)),
                          "tS" if kc == 15 else None)
                    wt_free[slot] = cnt["tS"]
                    W("s", "tS", tg)
                    for tok in db_scales:
                        Wt("s", tok)
                    W("s", "vS", gs_free[slot])
                    I("s", lambda e, ic=ic, slot=slot, pG=pG: e.activation(
                        gs_b[slot][:], pG[:], AF.Silu,
                        scale=sg_sb[:, ic:ic + 1]), "sS")
                    W("s", "tS", wt_free[slot])
                    psum_free[ic % 2] = I(
                        "s", lambda e, ic=ic, slot=slot, pU=pU: e.activation(
                            gu_b[slot][:], pU[:], AF.Identity,
                            scale=su_sb[:, ic:ic + 1]), "sS")
                    W("v", "sS", psum_free[ic % 2])
                    W("v", "tS", fuse_free[0])
                    gs_free[slot] = I("v", lambda e, icl=icl, slot=slot:
                                      e.tensor_tensor(
                        fuse[:].rearrange("p (c n) -> p c n", c=QI)[:, icl, :],
                        gs_b[slot][:], gu_b[slot][:], A.mult), "vS")
                fuse_done = cnt["vS"]
                W("v", "vS", acc_wm[0])
                W("t", "vS", fuse_done)
                Wt("t", dg3)
                for tt in range(4):
                    for dc in range(4):
                        idx = tt * 4 + dc
                        psum = pO1 if idx % 2 == 0 else pO2
                        W("t", "vS", po_free[idx % 2])
                        for icl in range(QI):
                            I("t", lambda e, icl=icl, tt=tt, dc=dc, psum=psum:
                              e.matmul(
                                psum[:],
                                fuse[:].rearrange("p (c n) -> p c n", c=QI)
                                [:, icl, tt * 128:(tt + 1) * 128],
                                wdnb[icl][:, dc * 512:(dc + 1) * 512],
                                start=(icl == 0), stop=(icl == QI - 1)),
                              "tS" if icl == QI - 1 else None)
                        W("v", "tS", cnt["tS"])
                        if qb == 0:
                            po_free[idx % 2] = I(
                                "v", lambda e, tt=tt, dc=dc, psum=psum:
                                e.tensor_copy(
                                    ffn_acc[tt][:, dc * 512:(dc + 1) * 512],
                                    psum[:]), "vS")
                        else:
                            po_free[idx % 2] = I(
                                "v", lambda e, tt=tt, dc=dc, psum=psum:
                                e.tensor_tensor(
                                    ffn_acc[tt][:, dc * 512:(dc + 1) * 512],
                                    ffn_acc[tt][:, dc * 512:(dc + 1) * 512],
                                    psum[:], A.add), "vS")
                for w in range(QI):
                    wdn_free[w] = cnt["tS"]
                fuse_free[0] = cnt["tS"]
                acc_wm[0] = cnt["vS"]
            for tt in range(4):
                slot = tt % 2
                W("y", "cc", sp + 1)
                W("y", "vS", ar_done[0])
                r0 = sp * C + tt * 128
                da = dS(lambda e, rr=tt * 128, sp=sp: e.dma_start(
                    out=ar_b[0][:], in_=cc_out[sp][rr:rr + 128, :]))
                dx = dS(lambda e, r0=r0: e.dma_start(
                    out=xo_b[0][:], in_=x_own[r0:r0 + 128, :]))
                Wt("v", da)
                Wt("v", dx)
                for tok in db_scales:
                    Wt("v", tok)
                Wt("v", yout_tok[slot])
                W("v", "vS", acc_wm[0])
                I("v", lambda e, tt=tt, slot=slot: e.tensor_tensor(
                    y_b[slot][:], ffn_acc[tt][:], sdbc[:], A.mult), "vS")
                W("v", "vS", cnt["vS"])
                I("v", lambda e, slot=slot: e.tensor_tensor(
                    y_b[slot][:], y_b[slot][:], bobc[:], A.add), "vS")
                W("v", "vS", cnt["vS"])
                I("v", lambda e, slot=slot: e.tensor_tensor(
                    y_b[slot][:], y_b[slot][:], xo_b[0][:], A.add), "vS")
                W("v", "vS", cnt["vS"])
                yv = I("v", lambda e, slot=slot: e.tensor_tensor(
                    y_b[slot][:], y_b[slot][:], ar_b[0][:], A.add), "vS")
                ar_done[0] = yv
                W("y", "vS", yv)
                yout_tok[slot] = dO(lambda e, slot=slot, r0=r0: e.dma_start(
                    out=y_d[r0:r0 + 128, :], in_=y_b[slot][:]))
                out_dmas[0] += 1

        for j in range(2):
            ops["y"].append(lambda e, j=j: e.wait_ge(sems[f"dO{j}"], cnt[f"dO{j}"]))

        with nc.Block() as block:
            @block.tensor
            def _(t):
                for f in ops["t"]:
                    f(t)

            @block.vector
            def _(v):
                for f in ops["v"]:
                    f(v)

            @block.scalar
            def _(s):
                for f in ops["s"]:
                    f(s)

            @block.gpsimd
            def _(g):
                for f in ops["g"]:
                    f(g)

            @block.sync
            def _(y):
                for f in ops["y"]:
                    f(y)

    return nc


def host_tables():
    pos = np.arange(S, dtype=np.float64)
    invf = 1.0 / THETA ** (np.arange(0, DK, 2, dtype=np.float64) / DK)
    j = np.arange(DK) // 2
    ang = pos[None, :] * invf[j][:, None]
    cos, sin = np.cos(ang), np.sin(ang)
    sbase = (np.arange(0, DK, 2, dtype=np.float64) + 0.4 * DK) / (1.4 * DK)
    scale = sbase[j][:, None] ** ((pos - S // 2) / SBASE)[None, :]
    ropes = np.concatenate([cos * scale, sin * scale, cos / scale, sin / scale],
                           axis=1).astype(np.float32)
    P = np.zeros((DK, DK), np.float64)
    for jj in range(0, DK, 2):
        P[jj, jj + 1] = -1.0
        P[jj + 1, jj] = 1.0
    PT = np.ascontiguousarray(P.T).astype(ml_dtypes.bfloat16)
    ident = np.eye(128, dtype=np.float32).astype(ml_dtypes.bfloat16)
    consts = np.ascontiguousarray(np.broadcast_to(np.array(
        [-1.0 / D, 1.0 / D, -1.0 / DK, 1.0 / DK, EPS], np.float32), (128, 5)))
    return ropes, PT, ident, consts


def decay_tables(heads):
    T4, gq4, gk4, gC4 = [], [], [], []
    p = np.arange(128, dtype=np.float64)
    for h in heads:
        g = 1.0 - 2.0 ** (-5.0 - float(h))
        d = np.arange(896, dtype=np.float64)[None, :] - 384.0 - p[:, None]
        T4.append(np.where(d >= 0, g ** np.maximum(d, 0.0), 0.0))
        gq4.append(np.stack([g ** (128.0 * t + p + 1.0) for t in range(4)], 1))
        gk4.append(np.stack([g ** (511.0 - 128.0 * kt - p) for kt in range(4)], 1))
        gC4.append(np.full(128, g ** C))
    return (np.concatenate(T4, 1).astype(np.float32),
            np.concatenate(gq4, 1).astype(np.float32),
            np.concatenate(gk4, 1).astype(np.float32),
            np.stack(gC4, 1).astype(np.float32))


_NC = None


def kernel(L0_attn_x, L0_attn_Wq, L0_attn_bq, L0_attn_Wk, L0_attn_bk,
           L0_attn_Wv, L0_attn_bv, L0_attn_Wg, L0_attn_bg, L0_attn_Wo,
           L0_attn_bo, L0_ffn_w_gate, L0_ffn_w_up, L0_ffn_w_down,
           L0_ffn_s_gate, L0_ffn_s_up, L0_ffn_s_down):
    global _NC
    if _NC is None:
        _NC = build_nc()
    nc = _NC
    ropes, PT, ident, consts = host_tables()
    x = np.ascontiguousarray(np.asarray(L0_attn_x, np.float32))
    f32 = lambda a: np.ascontiguousarray(np.asarray(a, np.float32))
    i8 = lambda a: np.ascontiguousarray(np.asarray(a, np.int8))
    col4 = lambda a, hg: np.ascontiguousarray(
        f32(a)[4 * hg * 128:(4 * hg + 4) * 128].reshape(4, 128).T)
    sg_c = np.ascontiguousarray(f32(L0_ffn_s_gate).reshape(64, 128).T)
    su_c = np.ascontiguousarray(f32(L0_ffn_s_up).reshape(64, 128).T)
    sd_bc = np.ascontiguousarray(np.broadcast_to(f32(L0_ffn_s_down), (128, D)))
    bo_bc = np.ascontiguousarray(np.broadcast_to(f32(L0_attn_bo), (128, D)))
    wg_full = i8(L0_ffn_w_gate)
    wu_full = i8(L0_ffn_w_up)
    wd_full = i8(L0_ffn_w_down)
    in_maps = []
    for c in range(8):
        b, hg, qt = c // 4, c % 4, c % 4
        heads = list(range(4 * hg, 4 * hg + 4))
        decayT, gq_c, gk_c, gC_c = decay_tables(heads)
        sl = slice(4 * hg * 128, (4 * hg + 4) * 128)
        in_maps.append(dict(
            x_full=x[b],
            x_own=np.ascontiguousarray(x[b, qt * KB:(qt + 1) * KB]),
            wq_s=np.ascontiguousarray(f32(L0_attn_Wq)[:, sl]),
            wk_s=np.ascontiguousarray(f32(L0_attn_Wk)[:, sl]),
            wv_s=np.ascontiguousarray(f32(L0_attn_Wv)[:, sl]),
            wg_s=np.ascontiguousarray(f32(L0_attn_Wg)[:, sl]),
            wo_s=np.ascontiguousarray(f32(L0_attn_Wo)[sl, :]),
            bq_c=col4(L0_attn_bq, hg), bk_c=col4(L0_attn_bk, hg),
            bv_c=col4(L0_attn_bv, hg), bg_c=col4(L0_attn_bg, hg),
            w_gate=wg_full, w_up=wu_full, w_down=wd_full,
            sg_c=sg_c, su_c=su_c, sd_bc=sd_bc, bo_bc=bo_bc,
            ropes=ropes, decayT=decayT, gq_c=gq_c, gk_c=gk_c, gC_c=gC_c,
            consts=consts, ident=ident, PT=PT,
        ))
    res = run_bass_kernel_spmd(nc, in_maps, core_ids=list(range(8)))
    out = np.empty((B, S, D), np.float32)
    for c in range(8):
        out[c // 4, (c % 4) * KB:(c % 4 + 1) * KB] = res.results[c]["y"]
    return out


# version 04 (final)
# speedup vs baseline: 1.0980x; 1.0086x over previous
import numpy as np
import ml_dtypes
import concourse.bass as bass
import concourse.mybir as mybir
from concourse.bass_utils import run_bass_kernel_spmd
from contextlib import ExitStack

F32 = mybir.dt.float32
BF16 = mybir.dt.bfloat16
I8 = mybir.dt.int8
A = mybir.AluOpType
AF = mybir.ActivationFunctionType
AX = mybir.AxisListType

B, S, D, H, DK, INTER = 2, 4096, 2048, 16, 128, 8192
C = 512
NCH = S // C
NT = C // 128
EPS = 1e-5
THETA = 1e4
SBASE = 512.0
KB = 1024
NCHD = NCH
NQD = 16
NSPD = 2
DBG_NO_U = False
DBG_U_LEVEL = 3
DBG_NO_SUPD = False


def build_nc():
    nc = bass.Bass("TRN2", target_bir_lowering=False, debug=False)

    def din(name, shape, dt):
        return nc.dram_tensor(name, shape, dt, kind="ExternalInput").ap()

    x_full = din("x_full", [S, D], F32)
    x_own = din("x_own", [KB, D], F32)
    wq_d = din("wq_s", [D, 512], F32)
    wk_d = din("wk_s", [D, 512], F32)
    wv_d = din("wv_s", [D, 512], F32)
    wg_d = din("wg_s", [D, 512], F32)
    wo_d = din("wo_s", [512, D], F32)
    bq_d = din("bq_c", [128, 4], F32)
    bk_d = din("bk_c", [128, 4], F32)
    bv_d = din("bv_c", [128, 4], F32)
    bg_d = din("bg_c", [128, 4], F32)
    wgate_d = din("w_gate", [D, INTER], I8)
    wup_d = din("w_up", [D, INTER], I8)
    wdn_d = din("w_down", [INTER, D], I8)
    sg_d = din("sg_c", [128, 64], F32)
    su_d = din("su_c", [128, 64], F32)
    sdbc_d = din("sd_bc", [128, D], F32)
    bobc_d = din("bo_bc", [128, D], F32)
    rope_d = din("ropes", [128, 4 * S], F32)
    decay_d = din("decayT", [128, 4 * 896], F32)
    gq_d = din("gq_c", [128, 16], F32)
    gk_d = din("gk_c", [128, 16], F32)
    gC_d = din("gC_c", [128, 4], F32)
    cst_d = din("consts", [128, 5], F32)
    id_d = din("ident", [128, 128], BF16)
    pt_d = din("PT", [128, 128], BF16)

    y_d = nc.dram_tensor("y", [KB, D], F32, kind="ExternalOutput").ap()
    cc_in_t = [nc.dram_tensor(f"cc_in{j}", [S // 2, D], BF16) for j in range(2)]
    cc_out_t = [nc.dram_tensor(f"cc_out{j}", [KB // 2, D], BF16) for j in range(2)]
    cc_in = [t.ap() for t in cc_in_t]
    cc_out = [t.ap() for t in cc_out_t]

    with ExitStack() as st:
        E = st.enter_context
        ARENA = 190 * KB
        arena = E(nc.sbuf_tensor("arena", [128, ARENA], I8))
        cur = [0]

        def sb(cols, dt):
            nb = cols * {F32: 4, BF16: 2, I8: 1}[dt]
            nb = (nb + 511) // 512 * 512
            off = cur[0]
            cur[0] += nb
            assert cur[0] <= ARENA, f"arena overflow {cur[0]}"
            return arena[:, off:off + cols * {F32: 4, BF16: 2, I8: 1}[dt]].bitcast(dt)

        xbuf = [sb(D, F32) for _ in range(2)]
        scr = sb(D, F32)
        msum = [sb(1, F32) for _ in range(2)]
        mneg = [sb(1, F32) for _ in range(2)]
        sdv = [sb(1, F32) for _ in range(2)]
        rstd = [sb(1, F32) for _ in range(2)]
        xn_bf = [sb(D, BF16) for _ in range(2)]
        cst = sb(5, F32)
        idn = sb(128, BF16)
        ptm = sb(128, BF16)
        bq = sb(4, F32); bk = sb(4, F32); bv = sb(4, F32); bg = sb(4, F32)
        gq = sb(16, F32); gk = sb(16, F32); gCc = sb(4, F32)
        mn128 = sb(1, F32); ss128 = sb(1, F32); sd128 = sb(1, F32); rs128 = sb(1, F32)
        scr128 = sb(128, F32)
        Sst = sb(4 * 128, F32)
        Stmp = sb(128, F32)
        Sbf = sb(4 * 128, BF16)
        shared_end = cur[0]

        xnT = sb(16 * C, BF16)
        wq_sb = sb(16 * 512, BF16)
        wk_sb = sb(16 * 512, BF16)
        wv_sb = sb(16 * 512, BF16)
        wg_sb = sb(16 * 512, BF16)
        wo_sb = sb(4 * D, BF16)
        ropeb = [sb(4 * C, F32)]
        decay = sb(4 * 896, BF16)
        q0 = sb(C, BF16); k0 = sb(C, BF16)
        qr = sb(C, BF16); kr = sb(C, BF16)
        tmp1 = sb(C, F32); tmp2 = sb(C, F32)
        scm = sb(4 * C, BF16)
        vT_sb = sb(C, BF16)
        v_a = sb(4 * 128, BF16)
        k_a = sb(4 * 128, BF16)
        u_s = sb(128, F32)
        rg = sb(128, F32)
        gn_bf = sb(128, BF16)
        gnT = sb(4 * C, BF16)
        gs = sb(C, F32)
        preT = sb(4 * C, BF16)
        pbuf = [sb(512, BF16) for _ in range(2)]
        assert cur[0] <= ARENA

        cur[0] = shared_end
        QI = 4
        xnT_sp = sb(16 * C, BF16)
        fuse = sb(QI * C, BF16)
        ffn_acc = [sb(D, F32) for _ in range(4)]
        wdnb = [sb(D, BF16) for _ in range(QI)]
        wgt = [sb(16 * 128, BF16) for _ in range(2)]
        wut = [sb(16 * 128, BF16) for _ in range(2)]
        sg_sb = sb(64, F32); su_sb = sb(64, F32)
        sdbc = sb(D, F32); bobc = sb(D, F32)
        gs_b = [sb(C, F32) for _ in range(2)]
        gu_b = [sb(C, F32) for _ in range(2)]
        xo_b = [sb(D, F32)]
        ar_b = [sb(D, BF16)]
        y_b = [sb(D, F32) for _ in range(2)]
        assert cur[0] <= ARENA, f"phase B overflow {cur[0]}"

        pP1 = E(nc.psum_tensor("pP1", [128, 512], F32))
        pP2 = E(nc.psum_tensor("pP2", [128, 512], F32))
        pS1 = E(nc.psum_tensor("pS1", [128, 512], F32))
        pS2 = E(nc.psum_tensor("pS2", [128, 512], F32))
        pM = E(nc.psum_tensor("pM", [128, 512], F32))
        pT = E(nc.psum_tensor("pT", [128, 512], BF16))
        pO1 = E(nc.psum_tensor("pO1", [128, 512], F32))
        pO2 = E(nc.psum_tensor("pO2", [128, 512], F32))

        semkeys = (["tS", "vS", "sS", "cc"]
                   + [f"dS{j}" for j in range(4)]
                   + [f"dG{j}" for j in range(4)]
                   + [f"dO{j}" for j in range(2)])
        sems = {k: E(nc.semaphore(k)) for k in semkeys}

        ops = {k: [] for k in "tvsgy"}
        cnt = {k: 0 for k in sems}
        waited = {}

        def W(eng, sk, val):
            if val is None or val <= 0:
                return
            if waited.get((eng, sk), 0) >= val:
                return
            waited[(eng, sk)] = val
            ops[eng].append(lambda e, sk=sk, v=val: e.wait_ge(sems[sk], v))

        def Wt(eng, tok):
            if tok is not None:
                W(eng, tok[0], tok[1])

        def I(eng, fn, sk=None, n=1):
            if sk:
                cnt[sk] += n

            def f(e, fn=fn, sk=sk, n=n):
                r = fn(e)
                if sk:
                    r.then_inc(sems[sk], n)
            ops[eng].append(f)
            return cnt[sk] if sk else None

        def mk_stream(qeng, prefix, depth):
            idx = [0]

            def issue(fn):
                j = idx[0] % depth
                sk = f"{prefix}{j}"
                if cnt[sk] > 0:
                    W(qeng, sk, cnt[sk])
                I(qeng, fn, sk, 16)
                idx[0] += 1
                return (sk, cnt[sk])
            return issue

        dS = mk_stream("y", "dS", 4)
        dG = mk_stream("g", "dG", 4)
        dO = mk_stream("y", "dO", 2)

        for t_, d_ in [(cst, cst_d), (idn, id_d), (ptm, pt_d),
                       (bq, bq_d), (bk, bk_d), (bv, bv_d), (bg, bg_d),
                       (gq, gq_d), (gk, gk_d), (gCc, gC_d)]:
            dS(lambda e, t_=t_, d_=d_: e.dma_start(out=t_[:], in_=d_))
        d_base = [(f"dS{j}", cnt[f"dS{j}"]) for j in range(4)]
        dG(lambda e: e.dma_start(
            out=decay[:].rearrange("p (h n) -> p h n", h=4),
            in_=decay_d.rearrange("p (h n) -> p h n", h=4)))
        for t_, d_ in [(wq_sb, wq_d), (wk_sb, wk_d), (wv_sb, wv_d), (wg_sb, wg_d)]:
            dG(lambda e, t_=t_, d_=d_: e.dma_start(
                out=t_[:].rearrange("p (c n) -> p c n", c=16),
                in_=d_.rearrange("(c p) n -> p c n", p=128)))
        dG(lambda e: e.dma_start(
            out=wo_sb[:].rearrange("p (h n) -> p h n", h=4),
            in_=wo_d.rearrange("(h p) n -> p h n", p=128)))
        dg_wproj = [(f"dG{j}", cnt[f"dG{j}"]) for j in range(4)]

        ln_done = {}
        ln_sq = {}
        xnbuf_free = {}
        rope_used = {}
        Sbf_done = [None] * 4
        pT_free = [None]
        po_free = [None, None]
        last_sc_evac = [None]

        def emit_ln(gt, src_dram, row0, slot, xnT_dst, col0):
            W("y", "vS", ln_done.get(gt - 2))
            W("y", "sS", ln_sq.get(gt - 2))
            dv = dS(lambda e, src_dram=src_dram, row0=row0, slot=slot:
                    e.dma_start(out=xbuf[slot][:],
                                in_=src_dram[row0:row0 + 128, :]))
            Wt("v", dv)
            I("v", lambda e, slot=slot: e.tensor_reduce(
                msum[slot][:], xbuf[slot][:], AX.X, A.add), "vS")
            W("s", "vS", cnt["vS"])
            for tok in d_base:
                Wt("s", tok)
            I("s", lambda e, slot=slot: e.activation(
                mneg[slot][:], msum[slot][:], AF.Identity, scale=cst[:, 0:1]), "sS")
            W("s", "sS", cnt["sS"])
            I("s", lambda e, slot=slot: e.activation(
                scr[:], xbuf[slot][:], AF.Square, bias=mneg[slot][:],
                accum_out=sdv[slot][:]), "sS")
            ln_sq[gt] = cnt["sS"]
            W("s", "sS", cnt["sS"])
            I("s", lambda e, slot=slot: e.activation(
                msum[slot][:], sdv[slot][:], AF.Sqrt, scale=cst[:, 1:2],
                bias=cst[:, 4:5]), "sS")
            W("v", "sS", cnt["sS"])
            I("v", lambda e, slot=slot: e.reciprocal(rstd[slot][:], msum[slot][:]),
              "vS")
            W("v", "vS", cnt["vS"])
            W("v", "tS", xnbuf_free.get(gt - 2))
            lv = I("v", lambda e, slot=slot: e.tensor_scalar(
                xn_bf[slot][:], xbuf[slot][:], mneg[slot][:], rstd[slot][:],
                A.add, A.mult), "vS")
            ln_done[gt] = lv
            for grp in range(4):
                for tok in d_base:
                    Wt("t", tok)
                W("t", "vS", lv)
                W("t", "vS", pT_free[0])
                for j in range(4):
                    jj = grp * 4 + j
                    I("t", lambda e, jj=jj, j=j, slot=slot: e.transpose(
                        pT[:, j * 128:(j + 1) * 128],
                        xn_bf[slot][:, jj * 128:(jj + 1) * 128], idn[:]),
                      "tS" if j == 3 else None)
                W("v", "tS", cnt["tS"])
                pT_free[0] = I("v", lambda e, grp=grp, xnT_dst=xnT_dst, col0=col0:
                               e.tensor_copy(
                    xnT_dst[:].rearrange("p (c n) -> p c n", c=16)
                    [:, grp * 4:grp * 4 + 4, col0:col0 + 128],
                    pT[:].rearrange("p (c n) -> p c n", c=4)), "vS")
            xnbuf_free[gt] = cnt["tS"]

        partial_t_prev = [None]
        last_supd = [None]
        pending_partials = []
        pbuf_free = [None, None]

        def emit_partials(ch, preT_done):
            W("t", "vS", preT_done)
            for tt in range(4):
                for dc in range(4):
                    idx = tt * 4 + dc
                    psum = pO1 if idx % 2 == 0 else pO2
                    W("t", "vS", po_free[idx % 2])
                    for h in range(4):
                        I("t", lambda e, h=h, tt=tt, dc=dc, psum=psum: e.matmul(
                            psum[:], preT[:, h * C + tt * 128: h * C + (tt + 1) * 128],
                            wo_sb[:, h * D + dc * 512: h * D + (dc + 1) * 512],
                            start=(h == 0), stop=(h == 3)), "tS" if h == 3 else None)
                    W("v", "tS", cnt["tS"])
                    slot = idx % 2
                    Wt("v", pbuf_free[slot])
                    pv = I("v", lambda e, psum=psum, slot=slot: e.tensor_copy(
                        pbuf[slot][:], psum[:]), "vS")
                    po_free[idx % 2] = pv
                    W("y", "vS", pv)
                    r0 = (ch // 2) * C + tt * 128
                    pbuf_free[slot] = dS(lambda e, slot=slot, r0=r0, dc=dc,
                                         par=ch % 2: e.dma_start(
                        out=cc_in[par][r0:r0 + 128, dc * 512:(dc + 1) * 512],
                        in_=pbuf[slot][:]))
            if ch == NCH - 2 and NCHD == NCH:
                for j in range(4):
                    W("g", f"dS{j}", cnt[f"dS{j}"])
                I("g", lambda e: e.collective_compute(
                    "ReduceScatter", A.add,
                    replica_groups=[[0, 1, 2, 3], [4, 5, 6, 7]],
                    ins=[cc_in_t[0].ap().opt()],
                    outs=[cc_out_t[0].ap().opt()]), "cc")
            partial_t_prev[0] = cnt["tS"]


        for ch in range(NCHD):
            W("y", "vS", rope_used.get(ch - 1))
            dr = dS(lambda e, ch=ch: e.dma_start(
                out=ropeb[0][:].rearrange("p (t n) -> p t n", t=4),
                in_=rope_d.rearrange("p (t n) -> p t n", t=4)
                [:, :, ch * C:(ch + 1) * C]))
            for ti in range(NT):
                gt = ch * NT + ti
                emit_ln(gt, x_full, gt * 128, gt % 2, xnT, ti * 128)
            while pending_partials:
                emit_partials(*pending_partials.pop(0))
            xnT_done = cnt["vS"]

            rv = ropeb[0][:].rearrange("p (t n) -> p t n", t=4)
            for h in range(4):
                for tok in dg_wproj:
                    Wt("t", tok)
                W("t", "vS", xnT_done)
                W("t", "sS", last_sc_evac[0])
                for kc in range(16):
                    I("t", lambda e, kc=kc, h=h: e.matmul(
                        pP1[:], wq_sb[:, kc * 512 + h * 128: kc * 512 + (h + 1) * 128],
                        xnT[:, kc * C:(kc + 1) * C],
                        start=(kc == 0), stop=(kc == 15)), "tS" if kc == 15 else None)
                W("s", "tS", cnt["tS"])
                eq = I("s", lambda e, h=h: e.activation(
                    q0[:], pP1[:], AF.Identity, bias=bq[:, h:h + 1]), "sS")
                for kc in range(16):
                    I("t", lambda e, kc=kc, h=h: e.matmul(
                        pP2[:], wk_sb[:, kc * 512 + h * 128: kc * 512 + (h + 1) * 128],
                        xnT[:, kc * C:(kc + 1) * C],
                        start=(kc == 0), stop=(kc == 15)), "tS" if kc == 15 else None)
                W("s", "tS", cnt["tS"])
                ek = I("s", lambda e, h=h: e.activation(
                    k0[:], pP2[:], AF.Identity, bias=bk[:, h:h + 1]), "sS")
                W("t", "sS", eq)
                I("t", lambda e: e.matmul(pP1[:], ptm[:], q0[:],
                                          start=True, stop=True), "tS")
                trq = cnt["tS"]
                W("t", "sS", ek)
                I("t", lambda e: e.matmul(pP2[:], ptm[:], k0[:],
                                          start=True, stop=True), "tS")
                trk = cnt["tS"]
                Wt("v", dr)
                W("v", "tS", trq)
                I("v", lambda e: e.tensor_tensor(tmp1[:], q0[:], rv[:, 0, :],
                                                 A.mult), "vS")
                I("v", lambda e: e.tensor_tensor(tmp2[:], pP1[:], rv[:, 1, :],
                                                 A.mult), "vS")
                W("v", "vS", cnt["vS"])
                vq = I("v", lambda e: e.tensor_tensor(qr[:], tmp1[:], tmp2[:],
                                                      A.add), "vS")
                W("v", "tS", trk)
                W("v", "vS", cnt["vS"])
                I("v", lambda e: e.tensor_tensor(tmp1[:], k0[:], rv[:, 2, :],
                                                 A.mult), "vS")
                I("v", lambda e: e.tensor_tensor(tmp2[:], pP2[:], rv[:, 3, :],
                                                 A.mult), "vS")
                W("v", "vS", cnt["vS"])
                vk = I("v", lambda e: e.tensor_tensor(kr[:], tmp1[:], tmp2[:],
                                                      A.add), "vS")
                rope_used[ch] = vk

                W("t", "vS", vq)
                for kc in range(16):
                    I("t", lambda e, kc=kc, h=h: e.matmul(
                        pP1[:], wv_sb[:, kc * 512 + h * 128: kc * 512 + (h + 1) * 128],
                        xnT[:, kc * C:(kc + 1) * C],
                        start=(kc == 0), stop=(kc == 15)), "tS" if kc == 15 else None)
                W("s", "tS", cnt["tS"])
                ev = I("s", lambda e, h=h: e.activation(
                    vT_sb[:], pP1[:], AF.Identity, bias=bv[:, h:h + 1]), "sS")
                W("t", "sS", ev)
                W("t", "vS", pT_free[0])
                for kt in range(4):
                    I("t", lambda e, kt=kt: e.transpose(
                        pT[:, kt * 128:(kt + 1) * 128],
                        vT_sb[:, kt * 128:(kt + 1) * 128], idn[:]),
                      "tS" if kt == 3 else None)
                W("v", "tS", cnt["tS"])
                pT_free[0] = I("v", lambda e: e.tensor_copy(v_a[:], pT[:]), "vS")
                va = pT_free[0]

                W("t", "vS", vk)
                msk_done = [None, None]
                for kt in range(4):
                    psum = pS1 if kt % 2 == 0 else pS2
                    W("t", "vS", msk_done[kt % 2])
                    I("t", lambda e, kt=kt, psum=psum: e.matmul(
                        psum[:], kr[:, kt * 128:(kt + 1) * 128], qr[:],
                        start=True, stop=True), "tS")
                    W("v", "tS", cnt["tS"])
                    off = 384 - 128 * kt
                    mv = I("v", lambda e, kt=kt, psum=psum, off=off, h=h:
                           e.tensor_tensor(
                        scm[:, kt * C:(kt + 1) * C], psum[:],
                        decay[:].rearrange("p (h n) -> p h n", h=4)
                        [:, h, off:off + 512], A.mult), "vS")
                    msk_done[kt % 2] = mv
                scm_done = cnt["vS"]

                W("t", "vS", va)
                for kt in range(4):
                    I("t", lambda e, kt=kt: e.transpose(
                        pT[:, kt * 128:(kt + 1) * 128],
                        kr[:, kt * 128:(kt + 1) * 128], idn[:]),
                      "tS" if kt == 3 else None)
                W("s", "tS", cnt["tS"])
                for kt in range(4):
                    I("s", lambda e, kt=kt, h=h: e.activation(
                        k_a[:, kt * 128:(kt + 1) * 128],
                        pT[:, kt * 128:(kt + 1) * 128],
                        AF.Identity, scale=gk[:, h * 4 + kt:h * 4 + kt + 1]), "sS")
                ka_done = cnt["sS"]
                pT_free[0] = None
                ka_sS = ka_done

                gn_last = None
                for tt in range(4):
                    if ch > 0 and not DBG_NO_U:
                        W("t", "vS", Sbf_done[h])
                        I("t", lambda e, tt=tt, h=h: e.matmul(
                            pS1[:, 0:128], qr[:, tt * 128:(tt + 1) * 128],
                            Sbf[:, h * 128:(h + 1) * 128],
                            start=True, stop=True), "tS")
                        W("v", "tS", cnt["tS"])
                        eu = I("v", lambda e, tt=tt, h=h: e.tensor_scalar(
                            u_s[:], pS1[:, 0:128],
                            gq[:, h * 4 + tt:h * 4 + tt + 1], None,
                            A.mult, A.bypass), "vS")
                    W("t", "vS", scm_done if tt == 0 else None)
                    for kt in range(tt + 1):
                        I("t", lambda e, kt=kt, tt=tt: e.matmul(
                            pM[:, 128:256],
                            scm[:, kt * C + tt * 128: kt * C + (tt + 1) * 128],
                            v_a[:, kt * 128:(kt + 1) * 128],
                            start=(kt == 0), stop=(kt == tt)),
                          "tS" if kt == tt else None)
                    W("v", "tS", cnt["tS"])
                    if ch > 0 and not DBG_NO_U:
                        W("v", "vS", eu)
                        I("v", lambda e: e.tensor_tensor(
                            rg[:], pM[:, 128:256], u_s[:], A.add), "vS")
                    else:
                        I("v", lambda e: e.tensor_copy(rg[:], pM[:, 128:256]), "vS")
                    W("v", "vS", cnt["vS"])
                    I("v", lambda e: e.tensor_reduce(mn128[:], rg[:], AX.X, A.add),
                      "vS")
                    W("s", "vS", cnt["vS"])
                    I("s", lambda e: e.activation(
                        mn128[:], mn128[:], AF.Identity, scale=cst[:, 2:3]), "sS")
                    W("s", "sS", cnt["sS"])
                    I("s", lambda e: e.activation(
                        scr128[:], rg[:], AF.Square, bias=mn128[:],
                        accum_out=ss128[:]), "sS")
                    W("s", "sS", cnt["sS"])
                    I("s", lambda e: e.activation(
                        sd128[:], ss128[:], AF.Sqrt, scale=cst[:, 3:4],
                        bias=cst[:, 4:5]), "sS")
                    W("v", "sS", cnt["sS"])
                    I("v", lambda e: e.reciprocal(rs128[:], sd128[:]), "vS")
                    W("v", "vS", cnt["vS"])
                    gv = I("v", lambda e: e.tensor_scalar(
                        gn_bf[:], rg[:], mn128[:], rs128[:], A.add, A.mult), "vS")
                    W("t", "vS", gv)
                    W("t", "sS", ka_sS)
                    I("t", lambda e: e.transpose(pT[:, 0:128], gn_bf[:], idn[:]),
                      "tS")
                    W("v", "tS", cnt["tS"])
                    gn_last = I("v", lambda e, h=h, tt=tt: e.tensor_copy(
                        gnT[:, h * C + tt * 128: h * C + (tt + 1) * 128],
                        pT[:, 0:128]), "vS")
                pT_free[0] = gn_last

                W("t", "sS", ka_done)
                W("t", "vS", last_supd[0])
                for kt in range(4):
                    I("t", lambda e, kt=kt: e.matmul(
                        pM[:, 256:384], k_a[:, kt * 128:(kt + 1) * 128],
                        v_a[:, kt * 128:(kt + 1) * 128],
                        start=(kt == 0), stop=(kt == 3)), "tS" if kt == 3 else None)
                st_mm = cnt["tS"]
                if ch > 0 and not DBG_NO_SUPD:
                    W("s", "vS", last_supd[0])
                    es = I("s", lambda e, h=h: e.activation(
                        Stmp[:], Sst[:, h * 128:(h + 1) * 128], AF.Identity,
                        scale=gCc[:, h:h + 1]), "sS")
                    W("v", "sS", es)
                    W("v", "tS", st_mm)
                    I("v", lambda e, h=h: e.tensor_tensor(
                        Sst[:, h * 128:(h + 1) * 128], Stmp[:], pM[:, 256:384],
                        A.add), "vS")
                else:
                    W("v", "tS", st_mm)
                    I("v", lambda e, h=h: e.tensor_copy(
                        Sst[:, h * 128:(h + 1) * 128], pM[:, 256:384]), "vS")
                last_supd[0] = cnt["vS"]
                W("v", "vS", cnt["vS"])
                Sbf_done[h] = I("v", lambda e, h=h: e.tensor_copy(
                    Sbf[:, h * 128:(h + 1) * 128],
                    Sst[:, h * 128:(h + 1) * 128]), "vS")

                for kc in range(16):
                    I("t", lambda e, kc=kc, h=h: e.matmul(
                        pP2[:], wg_sb[:, kc * 512 + h * 128: kc * 512 + (h + 1) * 128],
                        xnT[:, kc * C:(kc + 1) * C],
                        start=(kc == 0), stop=(kc == 15)), "tS" if kc == 15 else None)
                W("s", "tS", cnt["tS"])
                eg = I("s", lambda e, h=h: e.activation(
                    gs[:], pP2[:], AF.Silu, bias=bg[:, h:h + 1]), "sS")
                last_sc_evac[0] = eg
                W("v", "sS", eg)
                W("v", "tS", partial_t_prev[0])
                I("v", lambda e, h=h: e.tensor_tensor(
                    preT[:, h * C:(h + 1) * C], gs[:], gnT[:, h * C:(h + 1) * C],
                    A.mult), "vS")
            preT_done = cnt["vS"]
            pending_partials.append((ch, preT_done))

        while pending_partials:
            emit_partials(*pending_partials.pop(0))

        for j in range(4):
            W("g", f"dS{j}", cnt[f"dS{j}"])
        if NCHD != NCH:
            I("g", lambda e: e.collective_compute(
                "ReduceScatter", A.add,
                replica_groups=[[0, 1, 2, 3], [4, 5, 6, 7]],
                ins=[cc_in_t[0].ap().opt()],
                outs=[cc_out_t[0].ap().opt()]), "cc")
        I("g", lambda e: e.collective_compute(
            "ReduceScatter", A.add,
            replica_groups=[[0, 1, 2, 3], [4, 5, 6, 7]],
            ins=[cc_in_t[1].ap().opt()],
            outs=[cc_out_t[1].ap().opt()]), "cc")

        db_scales = [dS(lambda e, t_=t_, d_=d_: e.dma_start(out=t_[:], in_=d_))
                     for t_, d_ in [(sg_sb, sg_d), (su_sb, su_d),
                                    (sdbc, sdbc_d), (bobc, bobc_d)]]

        wgv = wgate_d.rearrange("(c p) n -> p c n", p=128)
        wuv = wup_d.rearrange("(c p) n -> p c n", p=128)
        wt_free = [None, None]
        wdn_free = [None] * QI
        fuse_free = [None]
        gs_free = [None, None]
        psum_free = [None, None]
        out_dmas = [0]
        ar_done = [None]
        yout_tok = [None, None]
        acc_wm = [0]
        NQ = 64 // QI

        for sp in range(NSPD):
            for ti in range(NT):
                emit_ln(200 + sp * NT + ti, x_own, sp * C + ti * 128, ti % 2,
                        xnT_sp, ti * 128)
            xnsp_done = cnt["vS"]
            for qb in range(min(NQ, NQD)):
                dg3 = None
                for icl in range(QI):
                    ic = qb * QI + icl
                    slot = ic % 2
                    pG = pP1 if ic % 2 == 0 else pS1
                    pU = pP2 if ic % 2 == 0 else pS2
                    W("g", "tS", wt_free[slot])
                    dG(lambda e, ic=ic, slot=slot: e.dma_start(
                        out=wgt[slot][:].rearrange("p (c n) -> p c n", c=16),
                        in_=wgv[:, :, ic * 128:(ic + 1) * 128]))
                    dg2 = dG(lambda e, ic=ic, slot=slot: e.dma_start(
                        out=wut[slot][:].rearrange("p (c n) -> p c n", c=16),
                        in_=wuv[:, :, ic * 128:(ic + 1) * 128]))
                    W("g", "tS", wdn_free[icl])
                    dg3 = dG(lambda e, ic=ic, icl=icl: e.dma_start(
                        out=wdnb[icl][:],
                        in_=wdn_d[ic * 128:(ic + 1) * 128, :]))
                    Wt("t", dg2)
                    W("t", "vS", xnsp_done)
                    W("t", "sS", psum_free[ic % 2])
                    for kc in range(16):
                        I("t", lambda e, kc=kc, slot=slot, pG=pG: e.matmul(
                            pG[:], wgt[slot][:, kc * 128:(kc + 1) * 128],
                            xnT_sp[:, kc * C:(kc + 1) * C],
                            start=(kc == 0), stop=(kc == 15)),
                          "tS" if kc == 15 else None)
                    tg = cnt["tS"]
                    for kc in range(16):
                        I("t", lambda e, kc=kc, slot=slot, pU=pU: e.matmul(
                            pU[:], wut[slot][:, kc * 128:(kc + 1) * 128],
                            xnT_sp[:, kc * C:(kc + 1) * C],
                            start=(kc == 0), stop=(kc == 15)),
                          "tS" if kc == 15 else None)
                    wt_free[slot] = cnt["tS"]
                    W("s", "tS", tg)
                    for tok in db_scales:
                        Wt("s", tok)
                    W("s", "vS", gs_free[slot])
                    I("s", lambda e, ic=ic, slot=slot, pG=pG: e.activation(
                        gs_b[slot][:], pG[:], AF.Silu,
                        scale=sg_sb[:, ic:ic + 1]), "sS")
                    W("s", "tS", wt_free[slot])
                    psum_free[ic % 2] = I(
                        "s", lambda e, ic=ic, slot=slot, pU=pU: e.activation(
                            gu_b[slot][:], pU[:], AF.Identity,
                            scale=su_sb[:, ic:ic + 1]), "sS")
                    W("v", "sS", psum_free[ic % 2])
                    W("v", "tS", fuse_free[0])
                    gs_free[slot] = I("v", lambda e, icl=icl, slot=slot:
                                      e.tensor_tensor(
                        fuse[:].rearrange("p (c n) -> p c n", c=QI)[:, icl, :],
                        gs_b[slot][:], gu_b[slot][:], A.mult), "vS")
                fuse_done = cnt["vS"]
                W("v", "vS", acc_wm[0])
                W("t", "vS", fuse_done)
                Wt("t", dg3)
                for tt in range(4):
                    for dc in range(4):
                        idx = tt * 4 + dc
                        psum = pO1 if idx % 2 == 0 else pO2
                        W("t", "vS", po_free[idx % 2])
                        for icl in range(QI):
                            I("t", lambda e, icl=icl, tt=tt, dc=dc, psum=psum:
                              e.matmul(
                                psum[:],
                                fuse[:].rearrange("p (c n) -> p c n", c=QI)
                                [:, icl, tt * 128:(tt + 1) * 128],
                                wdnb[icl][:, dc * 512:(dc + 1) * 512],
                                start=(icl == 0), stop=(icl == QI - 1)),
                              "tS" if icl == QI - 1 else None)
                        W("v", "tS", cnt["tS"])
                        if qb == 0:
                            po_free[idx % 2] = I(
                                "v", lambda e, tt=tt, dc=dc, psum=psum:
                                e.tensor_copy(
                                    ffn_acc[tt][:, dc * 512:(dc + 1) * 512],
                                    psum[:]), "vS")
                        else:
                            po_free[idx % 2] = I(
                                "v", lambda e, tt=tt, dc=dc, psum=psum:
                                e.tensor_tensor(
                                    ffn_acc[tt][:, dc * 512:(dc + 1) * 512],
                                    ffn_acc[tt][:, dc * 512:(dc + 1) * 512],
                                    psum[:], A.add), "vS")
                for w in range(QI):
                    wdn_free[w] = cnt["tS"]
                fuse_free[0] = cnt["tS"]
                acc_wm[0] = cnt["vS"]
            for tt in range(4):
                slot = tt % 2
                W("y", "cc", sp + 1)
                W("y", "vS", ar_done[0])
                r0 = sp * C + tt * 128
                da = dS(lambda e, rr=tt * 128, sp=sp: e.dma_start(
                    out=ar_b[0][:], in_=cc_out[sp][rr:rr + 128, :]))
                dx = dS(lambda e, r0=r0: e.dma_start(
                    out=xo_b[0][:], in_=x_own[r0:r0 + 128, :]))
                Wt("v", da)
                Wt("v", dx)
                for tok in db_scales:
                    Wt("v", tok)
                Wt("v", yout_tok[slot])
                W("v", "vS", acc_wm[0])
                I("v", lambda e, tt=tt, slot=slot: e.tensor_tensor(
                    y_b[slot][:], ffn_acc[tt][:], sdbc[:], A.mult), "vS")
                W("v", "vS", cnt["vS"])
                I("v", lambda e, slot=slot: e.tensor_tensor(
                    y_b[slot][:], y_b[slot][:], bobc[:], A.add), "vS")
                W("v", "vS", cnt["vS"])
                I("v", lambda e, slot=slot: e.tensor_tensor(
                    y_b[slot][:], y_b[slot][:], xo_b[0][:], A.add), "vS")
                W("v", "vS", cnt["vS"])
                yv = I("v", lambda e, slot=slot: e.tensor_tensor(
                    y_b[slot][:], y_b[slot][:], ar_b[0][:], A.add), "vS")
                ar_done[0] = yv
                W("y", "vS", yv)
                yout_tok[slot] = dO(lambda e, slot=slot, r0=r0: e.dma_start(
                    out=y_d[r0:r0 + 128, :], in_=y_b[slot][:]))
                out_dmas[0] += 1

        for j in range(2):
            ops["y"].append(lambda e, j=j: e.wait_ge(sems[f"dO{j}"], cnt[f"dO{j}"]))

        with nc.Block() as block:
            @block.tensor
            def _(t):
                for f in ops["t"]:
                    f(t)

            @block.vector
            def _(v):
                for f in ops["v"]:
                    f(v)

            @block.scalar
            def _(s):
                for f in ops["s"]:
                    f(s)

            @block.gpsimd
            def _(g):
                for f in ops["g"]:
                    f(g)

            @block.sync
            def _(y):
                for f in ops["y"]:
                    f(y)

    return nc


def host_tables():
    pos = np.arange(S, dtype=np.float64)
    invf = 1.0 / THETA ** (np.arange(0, DK, 2, dtype=np.float64) / DK)
    j = np.arange(DK) // 2
    ang = pos[None, :] * invf[j][:, None]
    cos, sin = np.cos(ang), np.sin(ang)
    sbase = (np.arange(0, DK, 2, dtype=np.float64) + 0.4 * DK) / (1.4 * DK)
    scale = sbase[j][:, None] ** ((pos - S // 2) / SBASE)[None, :]
    ropes = np.concatenate([cos * scale, sin * scale, cos / scale, sin / scale],
                           axis=1).astype(np.float32)
    P = np.zeros((DK, DK), np.float64)
    for jj in range(0, DK, 2):
        P[jj, jj + 1] = -1.0
        P[jj + 1, jj] = 1.0
    PT = np.ascontiguousarray(P.T).astype(ml_dtypes.bfloat16)
    ident = np.eye(128, dtype=np.float32).astype(ml_dtypes.bfloat16)
    consts = np.ascontiguousarray(np.broadcast_to(np.array(
        [-1.0 / D, 1.0 / D, -1.0 / DK, 1.0 / DK, EPS], np.float32), (128, 5)))
    return ropes, PT, ident, consts


def decay_tables(heads):
    T4, gq4, gk4, gC4 = [], [], [], []
    p = np.arange(128, dtype=np.float64)
    for h in heads:
        g = 1.0 - 2.0 ** (-5.0 - float(h))
        d = np.arange(896, dtype=np.float64)[None, :] - 384.0 - p[:, None]
        T4.append(np.where(d >= 0, g ** np.maximum(d, 0.0), 0.0))
        gq4.append(np.stack([g ** (128.0 * t + p + 1.0) for t in range(4)], 1))
        gk4.append(np.stack([g ** (511.0 - 128.0 * kt - p) for kt in range(4)], 1))
        gC4.append(np.full(128, g ** C))
    return (np.concatenate(T4, 1).astype(np.float32),
            np.concatenate(gq4, 1).astype(np.float32),
            np.concatenate(gk4, 1).astype(np.float32),
            np.stack(gC4, 1).astype(np.float32))


_NC = None


def kernel(L0_attn_x, L0_attn_Wq, L0_attn_bq, L0_attn_Wk, L0_attn_bk,
           L0_attn_Wv, L0_attn_bv, L0_attn_Wg, L0_attn_bg, L0_attn_Wo,
           L0_attn_bo, L0_ffn_w_gate, L0_ffn_w_up, L0_ffn_w_down,
           L0_ffn_s_gate, L0_ffn_s_up, L0_ffn_s_down):
    global _NC
    if _NC is None:
        _NC = build_nc()
    nc = _NC
    ropes, PT, ident, consts = host_tables()
    x = np.ascontiguousarray(np.asarray(L0_attn_x, np.float32))
    f32 = lambda a: np.ascontiguousarray(np.asarray(a, np.float32))
    i8 = lambda a: np.ascontiguousarray(np.asarray(a, np.int8))
    col4 = lambda a, hg: np.ascontiguousarray(
        f32(a)[4 * hg * 128:(4 * hg + 4) * 128].reshape(4, 128).T)
    sg_c = np.ascontiguousarray(f32(L0_ffn_s_gate).reshape(64, 128).T)
    su_c = np.ascontiguousarray(f32(L0_ffn_s_up).reshape(64, 128).T)
    sd_bc = np.ascontiguousarray(np.broadcast_to(f32(L0_ffn_s_down), (128, D)))
    bo_bc = np.ascontiguousarray(np.broadcast_to(f32(L0_attn_bo), (128, D)))
    wg_full = i8(L0_ffn_w_gate)
    wu_full = i8(L0_ffn_w_up)
    wd_full = i8(L0_ffn_w_down)
    in_maps = []
    for c in range(8):
        b, hg, qt = c // 4, c % 4, c % 4
        heads = list(range(4 * hg, 4 * hg + 4))
        decayT, gq_c, gk_c, gC_c = decay_tables(heads)
        sl = slice(4 * hg * 128, (4 * hg + 4) * 128)
        in_maps.append(dict(
            x_full=x[b],
            x_own=np.ascontiguousarray(x[b, qt * KB:(qt + 1) * KB]),
            wq_s=np.ascontiguousarray(f32(L0_attn_Wq)[:, sl]),
            wk_s=np.ascontiguousarray(f32(L0_attn_Wk)[:, sl]),
            wv_s=np.ascontiguousarray(f32(L0_attn_Wv)[:, sl]),
            wg_s=np.ascontiguousarray(f32(L0_attn_Wg)[:, sl]),
            wo_s=np.ascontiguousarray(f32(L0_attn_Wo)[sl, :]),
            bq_c=col4(L0_attn_bq, hg), bk_c=col4(L0_attn_bk, hg),
            bv_c=col4(L0_attn_bv, hg), bg_c=col4(L0_attn_bg, hg),
            w_gate=wg_full, w_up=wu_full, w_down=wd_full,
            sg_c=sg_c, su_c=su_c, sd_bc=sd_bc, bo_bc=bo_bc,
            ropes=ropes, decayT=decayT, gq_c=gq_c, gk_c=gk_c, gC_c=gC_c,
            consts=consts, ident=ident, PT=PT,
        ))
    res = run_bass_kernel_spmd(nc, in_maps, core_ids=list(range(8)))
    out = np.empty((B, S, D), np.float32)
    for c in range(8):
        out[c // 4, (c % 4) * KB:(c % 4 + 1) * KB] = res.results[c]["y"]
    return out
